# version 10
# baseline (speedup 1.0000x reference)
import numpy as np
import ml_dtypes
import concourse.bass as bass
import concourse.mybir as mybir
from concourse.bass_utils import run_bass_kernel_spmd

F32 = mybir.dt.float32
BF16 = mybir.dt.bfloat16
AF = mybir.ActivationFunctionType
ALU = mybir.AluOpType
AX = mybir.AxisListType

E = 2048
S = 16384
NCORES = 8
SCALE = 1.0 / float(np.sqrt(2048.0))
EPS = 1e-5
NEG = -1.0e9
RET1 = [1, 3, 5, 7, 9, 11, 13, 15]
DOWN_JG = [5, 5, 5, 5, 5, 5, 5, 5, 4]

ST_M = 0
ST_L = 8
ST_BM = 16
ST_NM = 24
ST_CORR = 32
ST_BS = 40
ST_RECIP = 48
ST_MEAN = 49
ST_VAR = 50
ST_LNV = 51
ST_RSTD = 52
ST2_MEAN = 0
ST2_VAR = 1
ST2_LNV = 2
ST2_RSTD = 3
ST2_BASE = (56, 73)
ST_CSV = 64
ST_EPS = 72


def _pat(c):
    return [c, 15 - c, 16 + c, 31 - c, 32 + c, 47 - c, 48 + c, 63 - c]


def _stage(inputs):
    x = np.asarray(inputs["L0_attn_x"], np.float32).reshape(S, E)
    wq = np.asarray(inputs["L0_attn_Wq"], np.float32)
    wk = np.asarray(inputs["L0_attn_Wk"], np.float32)
    wv = np.asarray(inputs["L0_attn_Wv"], np.float32)
    gw = np.asarray(inputs["L0_ffn_gate_w"], np.float32)
    uw = np.asarray(inputs["L0_ffn_up_w"], np.float32)
    dw = np.asarray(inputs["L0_ffn_down_w"], np.float32)
    gs = np.asarray(inputs["L0_ffn_gate_s"], np.float32)
    us = np.asarray(inputs["L0_ffn_up_s"], np.float32)
    ds = np.asarray(inputs["L0_ffn_down_s"], np.float32)

    C = np.ascontiguousarray
    shared = {
        "wqT": C(wq.reshape(16, 128, 16, 128).transpose(0, 3, 2, 1)),
        "wkT": C(wk.reshape(16, 128, 16, 128).transpose(0, 3, 2, 1)),
        "wvT": C(wv.reshape(E, 16, 128).transpose(1, 2, 0)),
        "gateT": C(gw.reshape(44, 128, 16, 128).transpose(0, 3, 2, 1)),
        "upT": C(uw.reshape(44, 128, 16, 128).transpose(0, 3, 2, 1)),
        "downT": C(dw.reshape(2, 1024, 44, 128).transpose(0, 2, 3, 1)),
        "gate_s": C(gs.reshape(44, 128).T),
        "up_s": C(us.reshape(44, 128).T),
        "down_s": C(np.broadcast_to(ds.reshape(1, E), (128, E))),
        "ident": np.eye(128, dtype=ml_dtypes.bfloat16),
    }
    tri = np.triu(np.ones((128, 128), np.float32), k=1) * NEG

    in_maps, row_maps = [], []
    for c in range(NCORES):
        tiles = [[64 * p + t for t in _pat(c)] for p in (0, 1)]
        rows = np.concatenate([
            np.arange(t * 128, t * 128 + 128) for p in (0, 1) for t in tiles[p]
        ])
        row_maps.append(rows)
        xq = C(x[rows])
        xqT = C(
            x[rows].reshape(2, 8, 128, 16, 128).transpose(0, 4, 3, 1, 2)
        ).reshape(2, 128, 16, 1024)
        kvrows = np.concatenate([
            np.arange(4096 * m + 512 * c, 4096 * m + 512 * c + 512)
            for m in range(4)
        ])
        xkvT = C(x[kvrows].reshape(4, 512, 16, 128).transpose(3, 2, 0, 1))
        masks = np.zeros((2, 8, 2, 128, 512), np.float32)
        for p in (0, 1):
            for j in range(8):
                t = tiles[p][j]
                ret = RET1[j] + 16 * p
                for which, kb in enumerate((ret - 1, ret)):
                    for sub in range(4):
                        kt = 4 * kb + sub
                        blk = masks[p, j, which, :, sub * 128:(sub + 1) * 128]
                        if kt == t:
                            blk[:] = tri
                        elif kt > t:
                            blk[:] = NEG
        m = dict(shared)
        m.update({"xq": xq, "xqT": xqT, "xkvT": xkvT, "masks": C(masks)})
        in_maps.append(m)
    return in_maps, row_maps


class Ctr:
    def __init__(self, handle):
        self.h = handle
        self.n = 0


class Prog:
    ENGS = ("pe", "act", "dve", "pool", "sp")

    def __init__(self):
        self.ops = {e: [] for e in self.ENGS}

    def on(self, eng, fn):
        self.ops[eng].append(fn)


def build_nc(debug=False, stop_after=None, flash_trunc=0):
    _ph_order = ["init", "kvk", "kv", "qproj0", "flash0", "ffn0",
                 "qproj1", "flash1", "ffn1"]
    _stop_idx = _ph_order.index(stop_after) if stop_after else 99

    def want(ph):
        return _ph_order.index(ph) <= _stop_idx

    nc = bass.Bass("TRN2", target_bir_lowering=False, debug=False)

    d_xq = nc.dram_tensor("xq", [2048, E], F32, kind="ExternalInput")
    d_xqT = nc.dram_tensor("xqT", [2, 128, 16, 1024], F32, kind="ExternalInput")
    d_xkvT = nc.dram_tensor("xkvT", [128, 16, 4, 512], F32, kind="ExternalInput")
    d_wqT = nc.dram_tensor("wqT", [16, 128, 16, 128], F32, kind="ExternalInput")
    d_wkT = nc.dram_tensor("wkT", [16, 128, 16, 128], F32, kind="ExternalInput")
    d_wvT = nc.dram_tensor("wvT", [16, 128, E], F32, kind="ExternalInput")
    d_gateT = nc.dram_tensor("gateT", [44, 128, 16, 128], F32, kind="ExternalInput")
    d_upT = nc.dram_tensor("upT", [44, 128, 16, 128], F32, kind="ExternalInput")
    d_downT = nc.dram_tensor("downT", [2, 44, 128, 1024], F32, kind="ExternalInput")
    d_gs = nc.dram_tensor("gate_s", [128, 44], F32, kind="ExternalInput")
    d_us = nc.dram_tensor("up_s", [128, 44], F32, kind="ExternalInput")
    d_ds = nc.dram_tensor("down_s", [128, E], F32, kind="ExternalInput")
    d_ident = nc.dram_tensor("ident", [128, 128], BF16, kind="ExternalInput")
    d_masks = nc.dram_tensor("masks", [2, 8, 2, 128, 512], F32, kind="ExternalInput")
    d_out = nc.dram_tensor("out", [2048, E], F32, kind="ExternalOutput")

    kg_in = [nc.dram_tensor(f"kgin{m}", [E, 512], BF16) for m in range(4)]
    vg_in = [nc.dram_tensor(f"vgin{m}", [512, E], BF16) for m in range(4)]
    kg_out = [nc.dram_tensor(f"kgout{m}", [8 * E, 512], BF16) for m in range(4)]
    vg_out = [nc.dram_tensor(f"vgout{m}", [8 * 512, E], BF16) for m in range(4)]
    h1_dram = nc.dram_tensor("h1d", [8 * 128, E], F32)
    x2_dram = nc.dram_tensor("x2d", [8 * 128, E], F32)

    dbg = {}
    if debug:
        dbg["kg0"] = nc.dram_tensor("dbg_kg0", [8 * E, 512], BF16, kind="ExternalOutput")
        dbg["vg0"] = nc.dram_tensor("dbg_vg0", [8 * 512, E], BF16, kind="ExternalOutput")
        dbg["q"] = nc.dram_tensor("dbg_q", [2, 128, 16, 1024], BF16, kind="ExternalOutput")
        dbg["h1"] = nc.dram_tensor("dbg_h1", [2, 8 * 128, E], F32, kind="ExternalOutput")
        dbg["oacc"] = nc.dram_tensor("dbg_oacc", [2, 128, 8, E], F32, kind="ExternalOutput")

    from contextlib import ExitStack
    with ExitStack() as es:
        BASE = (nc.sbuf_base + 31) // 32 * 32
        LIMIT = nc.SBUF_PARTITION_SIZE_BYTES

        def sb(name, off_kb, shape, dt):
            off = BASE + int(off_kb * 1024)
            sz = int(np.prod(shape[1:])) * mybir.dt.size(dt)
            assert off + sz <= LIMIT, (name, off, sz, LIMIT)
            return nc.alloc_sbuf_tensor_at(name, shape, dt, offset=off)

        ident = sb("identt", 0.0, [128, 128], BF16)
        stats = sb("stats", 256 / 1024.0, [128, 80], F32)
        gs_sb = sb("gs_sb", 576 / 1024.0, [128, 44], F32)
        us_sb = sb("us_sb", 768 / 1024.0, [128, 44], F32)
        xkvT_bf = sb("xkvT_bf", 1, [128, 16, 4, 512], BF16)
        o_acc = sb("o_acc", 1, [128, 8, E], F32)
        hdn_lo = sb("hdn_lo", 1, [128, 32, 1024], BF16)
        wk_sb = sb("wk_sb", 65, [128, 16, 16, 128], BF16)
        wv_sb = sb("wv_sb", 65, [128, 16, E], BF16)
        k_blk = [sb("k_blk0", 65, [128, 16, 512], BF16),
                 sb("k_blk1", 81, [128, 16, 512], BF16)]
        v_blk = [sb("v_blk0", 97, [128, 4, E], BF16),
                 sb("v_blk1", 113, [128, 4, E], BF16)]
        h1T = sb("h1T", 65, [128, 16, 1024], BF16)
        x2part = sb("x2part", 65, [128, 8, 1024], F32)
        silu_sb = [sb("silu0", 97, [128, 1024], F32),
                   sb("silu1", 101, [128, 1024], F32)]
        u1_sb = [sb("u10", 105, [128, 1024], F32),
                 sb("u11", 109, [128, 1024], F32)]
        x2ld2 = [sb("x2ld0", 97, [128, E], F32),
                 sb("x2ld1", 105, [128, E], F32)]
        hdn_hi = sb("hdn_hi", 113, [128, 12, 1024], BF16)
        kstage2 = [sb("kstage0", 137, [128, 16, 512], BF16),
                   sb("kstage1", 153, [128, 16, 512], BF16),
                   sb("kstage2", 169, [128, 16, 512], BF16),
                   sb("kstage3", 185, [128, 16, 512], BF16)]
        vstage2 = [sb("vstage0", 137, [128, 4, E], BF16),
                   sb("vstage1", 153, [128, 4, E], BF16),
                   sb("vstage2", 169, [128, 4, E], BF16),
                   sb("vstage3", 185, [128, 4, E], BF16)]
        xqT_bf = sb("xqT_bf", 137, [128, 16, 1024], BF16)
        mask_sb = [sb("mask0", 137, [128, 2, 512], F32),
                   sb("mask1", 141, [128, 2, 512], F32)]
        xq_t = sb("xq_t", 145, [128, E], F32)
        h1st = sb("h1st", 153, [128, E], F32)
        p_sb = [sb("p_sb0", 161, [128, 512], BF16),
                sb("p_sb1", 162, [128, 512], BF16)]
        pt_sb = [sb("pt_sb0", 163, [128, 4, 128], BF16),
                 sb("pt_sb1", 164, [128, 4, 128], BF16)]
        gu_blk = [sb("gu0", 137, [128, 2, 16, 128], BF16),
                  sb("gu1", 145, [128, 2, 16, 128], BF16)]
        ds_b = sb("ds_b", 153, [128, E], F32)
        outst = sb("outst", 161, [128, E], F32)
        w_blk = [sb("w_blk0", 113, [128, 16, 128], BF16),
                 sb("w_blk1", 117, [128, 16, 128], BF16),
                 sb("w_blk2", 121, [128, 16, 128], BF16),
                 sb("w_blk3", 125, [128, 16, 128], BF16)]
        q_sb = sb("q_sb", 169, [128, 16, 1024], BF16)
        down_bf = [sb("down0", 169, [128, 5, 1024], BF16),
                   sb("down1", 179, [128, 5, 1024], BF16)]
        h1sl2 = [sb("h1sl0", 189, [128, 1024], F32),
                 sb("h1sl1", 201, [128, 1024], F32)]
        x2buf = [sb("x2buf0", 193, [128, 1024], F32),
                 sb("x2buf1", 197, [128, 1024], F32)]
        bnbuf = sb("bnbuf", 205, [128, 4, 24], F32)

        ps_a = es.enter_context(nc.psum_tensor("ps_a", [128, 2, 512], F32))
        ps_t = [es.enter_context(nc.psum_tensor("ps_t0", [128, 1024], BF16)),
                es.enter_context(nc.psum_tensor("ps_t1", [128, 1024], BF16))]
        ps_c = es.enter_context(nc.psum_tensor("ps_c", [128, 4, 512], F32))

        def sem(name):
            return Ctr(es.enter_context(nc.semaphore(name)))

        s_pe = sem("s_pe")
        s_act = sem("s_act")
        s_dve = sem("s_dve")
        s_skv4 = [sem(f"s_skv{m}") for m in range(4)]
        s_sh1 = sem("s_sh1")
        s_sx2b = [sem("s_sx2a"), sem("s_sx2b")]
        s_sout = sem("s_sout")
        s_junk = sem("s_junk")
        s_bar = sem("s_bar")
        s_ldc = sem("s_ldc")
        s_wk16 = [sem(f"s_wk{i}") for i in range(16)]
        s_xkv2 = [sem(f"s_xkv{m}") for m in range(4)]
        s_wv16 = [sem(f"s_wv{i}") for i in range(16)]
        s_xqt16 = [sem(f"s_xqt{i}") for i in range(16)]
        s_xqt = sem("s_xqt")
        s_wq2 = [sem(f"s_wqq{i}") for i in range(4)]
        s_h1t2 = [sem("s_h1t0"), sem("s_h1t1")]
        s_gu2 = [sem("s_gu0"), sem("s_gu1")]
        s_ds = sem("s_ds")
        s_dn2 = [sem("s_dn0"), sem("s_dn1")]
        s_k2 = [sem("s_k0"), sem("s_k1")]
        s_v2 = [sem("s_v0"), sem("s_v1")]
        s_mk2 = [sem("s_mk0"), sem("s_mk1")]
        s_xq = sem("s_xq")
        s_h1s2 = [sem("s_h1sa"), sem("s_h1sb")]
        s_x2l2 = [sem("s_x2la"), sem("s_x2lb")]
        gk = [sem(f"gk{m}") for m in range(4)]
        gv = [sem(f"gv{m}") for m in range(4)]

        P = Prog()

        def _emit_final_and_block():
            def sp_final(sp, t_out=s_sout.n, t_h1=s_sh1.n,
                         t_x2=(s_sx2b[0].n, s_sx2b[1].n),
                         skv=tuple(x.n for x in s_skv4)):
                sp.wait_ge(s_sout.h, t_out)
                sp.wait_ge(s_sh1.h, t_h1)
                sp.wait_ge(s_sx2b[0].h, t_x2[0])
                sp.wait_ge(s_sx2b[1].h, t_x2[1])
                for m4 in range(4):
                    sp.wait_ge(s_skv4[m4].h, skv[m4])
            P.on("sp", sp_final)

            block = es.enter_context(nc.Block())

            @block.tensor
            def _(t):
                for fn in P.ops["pe"]:
                    fn(t)

            @block.scalar
            def _(s):
                for fn in P.ops["act"]:
                    fn(s)

            @block.vector
            def _(v):
                for fn in P.ops["dve"]:
                    fn(v)

            @block.gpsimd
            def _(g):
                for fn in P.ops["pool"]:
                    fn(g)

            @block.sync
            def _(sp):
                for fn in P.ops["sp"]:
                    fn(sp)

        def barrier():
            s_bar.n += 5
            tgt = s_bar.n

            def fn_sp(eng, tgt=tgt, so=s_sout.n, sh=s_sh1.n,
                      sx=(s_sx2b[0].n, s_sx2b[1].n),
                      skv=tuple(x.n for x in s_skv4)):
                eng.wait_ge(s_sout.h, so)
                eng.wait_ge(s_sh1.h, sh)
                eng.wait_ge(s_sx2b[0].h, sx[0])
                eng.wait_ge(s_sx2b[1].h, sx[1])
                for m4 in range(4):
                    eng.wait_ge(s_skv4[m4].h, skv[m4])
                eng.drain()
                eng.sem_inc(s_bar.h, 1)
                eng.wait_ge(s_bar.h, tgt)
            P.on("sp", fn_sp)
            for e in ("pe", "act", "dve", "pool"):
                def fn(eng, tgt=tgt):
                    eng.drain()
                    eng.sem_inc(s_bar.h, 1)
                    eng.wait_ge(s_bar.h, tgt)
                P.on(e, fn)

        def init_loads(g):
            g.dma_start(out=ident[:, :], in_=d_ident[:, :]).then_inc(s_ldc.h, 16)
            g.dma_start(out=gs_sb[:, :], in_=d_gs[:, :]).then_inc(s_ldc.h, 16)
            g.dma_start(out=us_sb[:, :], in_=d_us[:, :]).then_inc(s_ldc.h, 16)
            g.memset(stats[:, ST_EPS:ST_EPS + 1], EPS)
            g.dma_start(out=xkvT_bf[:, :, 0, :],
                        in_=d_xkvT[:, :, 0, :]).then_inc(s_xkv2[0].h, 16)
            for dc in range(16):
                g.dma_start(
                    out=wk_sb[:, dc, :, :], in_=d_wkT[dc, :, :, :]
                ).then_inc(s_wk16[dc].h, 16)
            for m in range(1, 4):
                g.dma_start(out=xkvT_bf[:, :, m, :],
                            in_=d_xkvT[:, :, m, :]).then_inc(s_xkv2[m].h, 16)
        P.on("pool", init_loads)
        s_ldc.n += 48
        for i in range(16):
            s_wk16[i].n += 16
        for m in range(4):
            s_xkv2[m].n += 16
        KV_IN_READY = s_ldc.n

        kev = {}
        kst_dma = {}
        for m in range(4 if want("kvk") else 0):
            for dc in range(16):
                bank = dc % 2
                guard = kev.get((m, dc - 2)) or kev.get((m - 1, 14 + dc))

                def pe_k(t, m=m, dc=dc, bank=bank, guard=guard):
                    if dc == 0:
                        t.wait_ge(s_xkv2[m].h, 16)
                        if m == 0:
                            t.wait_ge(s_ldc.h, KV_IN_READY)
                    if m == 0:
                        t.wait_ge(s_wk16[dc].h, 16)
                    if guard:
                        t.wait_ge(s_act.h, guard)
                    for ec in range(16):
                        mm = t.matmul(ps_a[:, bank, :], wk_sb[:, dc, ec, :],
                                      xkvT_bf[:, ec, m, :],
                                      start=(ec == 0), stop=(ec == 15))
                    mm.then_inc(s_pe.h, 1)
                P.on("pe", pe_k)
                s_pe.n += 1
                pm = s_pe.n
                def act_k(s, m=m, dc=dc, bank=bank, pm=pm):
                    s.wait_ge(s_pe.h, pm)
                    s.copy(kstage2[m][:, dc, :],
                           ps_a[:, bank, :]).then_inc(s_act.h, 1)
                P.on("act", act_k)
                s_act.n += 1
                kev[(m, dc)] = s_act.n

            am = s_act.n

            def sp_kst(sp, m=m, am=am):
                sp.wait_ge(s_act.h, am)
                sp.dma_start(
                    out=kg_in[m].ap().rearrange("(dc p) t -> p dc t", p=128),
                    in_=kstage2[m][:, :, :],
                ).then_inc(s_skv4[m].h, 16)
            P.on("sp", sp_kst)
            s_skv4[m].n += 16
            kst_dma[m] = (m, s_skv4[m].n)

            if m == 3:
                K_PE_DONE = s_pe.n

                def v_loads(g, kpd=K_PE_DONE):
                    g.wait_ge(s_pe.h, kpd)
                    for ec in range(16):
                        g.dma_start(
                            out=wv_sb[:, ec, :], in_=d_wvT[ec, :, :],
                        ).then_inc(s_wv16[ec].h, 16)
                P.on("pool", v_loads)
                for i in range(16):
                    s_wv16[i].n += 16

            def pool_gk(g, m=m, tgt=(m, s_skv4[m].n)):
                g.wait_ge(s_skv4[tgt[0]].h, tgt[1])
                g.collective_compute(
                    "AllGather", ALU.bypass,
                    replica_groups=[list(range(NCORES))],
                    ins=[kg_in[m].ap().opt()],
                    outs=[kg_out[m].ap().opt()],
                ).then_inc(gk[m].h)
            P.on("pool", pool_gk)
            gk[m].n += 1

        if not want("kv"):
            barrier()
            _emit_final_and_block()
            return nc

        vev = {}
        vst_dma = {}
        vidx = 0
        for m in range(4):
            for tc in range(4):
                for dh in range(2):
                    bp = vidx % 2
                    guard = vev.get(vidx - 2)
                    vidx += 1

                    def pe_v(t, m=m, tc=tc, dh=dh, bp=bp, guard=guard,
                             first=(m == 0 and tc == 0 and dh == 0)):
                        if guard:
                            t.wait_ge(s_act.h, guard)
                        n = 0
                        for ec in range(16):
                            if first:
                                t.wait_ge(s_wv16[ec].h, 16)
                            for dq in range(2):
                                n += 1
                                mm = t.matmul(
                                    ps_c[:, 2 * bp + dq, :],
                                    xkvT_bf[:, ec, m, tc * 128:(tc + 1) * 128],
                                    wv_sb[:, ec, dh * 1024 + dq * 512:
                                          dh * 1024 + (dq + 1) * 512],
                                    start=(ec == 0), stop=(ec == 15),
                                    skip_group_check=True)
                                if n == 16:
                                    mm.then_inc(s_junk.h, 1)
                        mm.then_inc(s_pe.h, 1)
                    P.on("pe", pe_v)
                    s_junk.n += 1
                    s_pe.n += 1
                    pm = s_pe.n
                    stg = kst_dma.get(m) if (tc == 0 and dh == 0) else None

                    def act_v(s, m=m, tc=tc, dh=dh, bp=bp, pm=pm, stg=stg):
                        s.wait_ge(s_pe.h, pm)
                        if stg:
                            s.wait_ge(s_skv4[stg[0]].h, stg[1])
                        s.copy(
                            vstage2[m][:, tc, dh * 1024:(dh + 1) * 1024],
                            ps_c[:, 2 * bp:2 * bp + 2, :],
                        ).then_inc(s_act.h, 1)
                    P.on("act", act_v)
                    s_act.n += 1
                    vev[vidx - 1] = s_act.n

            am = s_act.n

            def sp_vst(sp, m=m, am=am):
                sp.wait_ge(s_act.h, am)
                sp.dma_start(
                    out=vg_in[m].ap().rearrange("(tc p) d -> p tc d", p=128),
                    in_=vstage2[m][:, :, :],
                ).then_inc(s_skv4[m].h, 16)
            P.on("sp", sp_vst)
            s_skv4[m].n += 16
            vst_dma[m] = (m, s_skv4[m].n)

            if m == 3 and want("qproj0"):
                def pool_xqt0(g):
                    g.wait_ge(s_skv4[0].h, 32)
                    g.wait_ge(s_skv4[1].h, 32)
                    for ec in range(16):
                        g.dma_start(out=xqT_bf[:, ec, :],
                                    in_=d_xqT[0, :, ec, :]).then_inc(
                                        s_xqt16[ec].h, 16)
                P.on("pool", pool_xqt0)
                for i in range(16):
                    s_xqt16[i].n += 16

            def pool_gv(g, m=m, tgt=(m, s_skv4[m].n)):
                g.wait_ge(s_skv4[tgt[0]].h, tgt[1])
                g.collective_compute(
                    "AllGather", ALU.bypass,
                    replica_groups=[list(range(NCORES))],
                    ins=[vg_in[m].ap().opt()],
                    outs=[vg_out[m].ap().opt()],
                ).then_inc(gv[m].h)
            P.on("pool", pool_gv)
            gv[m].n += 1

        if debug:
            def dbg_g(sp):
                sp.wait_ge(gk[0].h, 1)
                sp.dma_start(out=dbg["kg0"].ap(), in_=kg_out[0].ap()).then_inc(s_sout.h, 16)
                sp.wait_ge(gv[0].h, 1)
                sp.dma_start(out=dbg["vg0"].ap(), in_=vg_out[0].ap()).then_inc(s_sout.h, 16)
            P.on("sp", dbg_g)
            s_sout.n += 32

        barrier()

        def emit_pass(p):
            NKB = 16 * (p + 1)
            ret = [RET1[j] + 16 * p for j in range(8)]

            def q_loads(g, p=p):
                for ec in range(16):
                    if p == 0:
                        break
                    g.dma_start(out=xqT_bf[:, ec, :],
                                in_=d_xqT[p, :, ec, :]).then_inc(s_xqt16[ec].h, 16)
                g.memset(stats[:, ST_M:ST_M + 8], -1.0e30)
                g.memset(stats[:, ST_L:ST_L + 8], 0.0)
                g.memset(o_acc[:, :, :], 0.0)
            P.on("pool", q_loads)
            if p == 1:
                for i in range(16):
                    s_xqt16[i].n += 16
            XQT_TGT = 16 * (p + 1)

            wq_pe = {}
            qev = {}
            wq_ld = {}
            for dc in range(16):
                guard = wq_pe.get(dc - 4)

                def pool_wq(g, dc=dc, guard=guard):
                    if guard:
                        g.wait_ge(s_pe.h, guard)
                    g.dma_start(
                        out=w_blk[dc % 4][:, :, :], in_=d_wqT[dc, :, :, :]
                    ).then_inc(s_wq2[dc % 4].h, 16)
                P.on("pool", pool_wq)
                s_wq2[dc % 4].n += 16
                wq_ld[dc] = (dc % 4, s_wq2[dc % 4].n)

                evguard = qev.get(dc - 2)

                def pe_q(t, dc=dc, ldm=wq_ld[dc], evg=evguard, xb=XQT_TGT):
                    t.wait_ge(s_wq2[ldm[0]].h, ldm[1])
                    if evg:
                        t.wait_ge(s_act.h, evg)
                    bp = dc % 2
                    n = 0
                    for ec in range(16):
                        if dc == 0:
                            t.wait_ge(s_xqt16[ec].h, xb)
                        for qh in range(2):
                            n += 1
                            mm = t.matmul(
                                ps_c[:, 2 * bp + qh, :],
                                w_blk[dc % 4][:, ec, :],
                                xqT_bf[:, ec, qh * 512:(qh + 1) * 512],
                                start=(ec == 0), stop=(ec == 15),
                                skip_group_check=True)
                            if n == 16:
                                mm.then_inc(s_junk.h, 1)
                    mm.then_inc(s_pe.h, 1)
                P.on("pe", pe_q)
                s_junk.n += 1
                s_pe.n += 1
                wq_pe[dc] = s_pe.n

                def act_q(s, dc=dc, pm=s_pe.n):
                    s.wait_ge(s_pe.h, pm)
                    s.activation(
                        q_sb[:, dc, :],
                        ps_c[:, 2 * (dc % 2):2 * (dc % 2) + 2, :].rearrange(
                            "pp a b -> pp (a b)"),
                        AF.Copy, scale=SCALE,
                    ).then_inc(s_act.h, 1)
                P.on("act", act_q)
                s_act.n += 1
                qev[dc] = s_act.n

            if debug:
                def dbg_q(sp, p=p, am=s_act.n):
                    sp.wait_ge(s_act.h, am)
                    sp.dma_start(out=dbg["q"][p, :, :, :], in_=q_sb[:, :, :]).then_inc(s_sout.h, 16)
                P.on("sp", dbg_q)
                s_sout.n += 16

            kld, vld = {}, {}
            kld_guard, vld_guard = {}, {}

            def emit_kv_load(kb, p=p):
                m, r = kb // 8, kb % 8
                pe_now = s_pe.n

                def sp_k(sp, kb=kb, m=m, r=r, pe_now=pe_now):
                    sp.wait_ge(gk[m].h, 1)
                    if kb >= 2:
                        sp.wait_ge(s_pe.h, kld_guard[kb - 2])
                    else:
                        sp.wait_ge(s_pe.h, pe_now)
                    sp.dma_start(
                        out=k_blk[kb % 2][:, :, :],
                        in_=kg_out[m][r * E:(r + 1) * E, :].rearrange(
                            "(dc pp) t -> pp dc t", pp=128),
                    ).then_inc(s_k2[kb % 2].h, 16)
                P.on("sp", sp_k)
                s_k2[kb % 2].n += 16
                kld[kb] = (kb % 2, s_k2[kb % 2].n)

                def sp_v(sp, kb=kb, m=m, r=r, pe_now=pe_now):
                    sp.wait_ge(gv[m].h, 1)
                    if kb >= 2:
                        sp.wait_ge(s_pe.h, vld_guard[kb - 2])
                    else:
                        sp.wait_ge(s_pe.h, pe_now)
                    sp.dma_start(
                        out=v_blk[kb % 2][:, :, :],
                        in_=vg_out[m][r * 512:(r + 1) * 512, :].rearrange(
                            "(tc pp) d -> pp tc d", pp=128),
                    ).then_inc(s_v2[kb % 2].h, 16)
                P.on("sp", sp_v)
                s_v2[kb % 2].n += 16
                vld[kb] = (kb % 2, s_v2[kb % 2].n)

            if p == 0 and want("flash0"):
                emit_kv_load(0)
                emit_kv_load(1)

            barrier()

            if not want(f"flash{p}"):
                return False

            updates = []
            for kb in range(NKB):
                for j in range(8):
                    if ret[j] >= kb:
                        w = 0 if kb == ret[j] - 1 else (1 if kb == ret[j] else None)
                        updates.append((kb, j, w))
            if flash_trunc:
                updates = updates[:flash_trunc]
                updates = [(kb, j, None) for (kb, j, w) in updates]
                ret = [99] * 8
            NU = len(updates)

            sc_mark = [0] * NU
            tr_mark = [0] * NU
            av_mark = [0] * NU
            exp_mark = [0] * NU
            ptv_mark = [0] * NU
            nm_mark = [0] * NU
            oup_mark = [0] * NU
            msk_mark = [0] * NU
            mask_loaded = {}
            mask_guard = {}
            xq_loaded = {}
            xq_guard = [0]
            oup_last = {}
            rstd_ready = {}
            h1_ready = {}
            h1_written = [0]
            epi_queue = []

            def emit_mask_load(j, p=p):
                g2 = mask_guard.get(j - 2, 0)

                def sp_m(sp, j=j, p=p, g2=g2):
                    if g2:
                        sp.wait_ge(s_dve.h, g2)
                    sp.dma_start(
                        out=mask_sb[j % 2][:, :, :],
                        in_=d_masks[p, j, :, :, :].rearrange("k pp t -> pp k t"),
                    ).then_inc(s_mk2[j % 2].h, 16)
                P.on("sp", sp_m)
                s_mk2[j % 2].n += 16
                mask_loaded[j] = (j % 2, s_mk2[j % 2].n)

            def emit_xq_load(j, p=p):
                g2 = xq_guard[-1]

                def sp_x(sp, j=j, p=p, g2=g2):
                    if g2:
                        sp.wait_ge(s_dve.h, g2)
                    sp.dma_start(
                        out=xq_t[:, :],
                        in_=d_xq[(p * 8 + j) * 128:(p * 8 + j + 1) * 128, :],
                    ).then_inc(s_xq.h, 16)
                P.on("sp", sp_x)
                s_xq.n += 16
                xq_loaded[j] = s_xq.n

            def emit_scores(i):
                kb, j, w = updates[i]
                if kb not in kld:
                    emit_kv_load(kb)

                def pe_sc(t, i=i, kb=kb, j=j):
                    t.wait_ge(s_k2[kld[kb][0]].h, kld[kb][1])
                    if i >= 2:
                        t.wait_ge(s_act.h, exp_mark[i - 2])
                    for dc in range(16):
                        mm = t.matmul(ps_a[:, i % 2, :],
                                      q_sb[:, dc, j * 128:(j + 1) * 128],
                                      k_blk[kb % 2][:, dc, :],
                                      start=(dc == 0), stop=(dc == 15))
                    mm.then_inc(s_pe.h, 1)
                P.on("pe", pe_sc)
                s_pe.n += 1
                sc_mark[i] = s_pe.n
                kld_guard[kb] = s_pe.n

            def emit_tr(i):
                def pe_tr(t, i=i):
                    t.wait_ge(s_act.h, exp_mark[i])
                    if i >= 2:
                        t.wait_ge(s_act.h, ptv_mark[i - 2])
                    for kc in range(4):
                        tr = t.transpose(ps_t[i % 2][:, kc * 128:(kc + 1) * 128],
                                         p_sb[i % 2][:, kc * 128:(kc + 1) * 128],
                                         ident[:, :])
                    tr.then_inc(s_pe.h, 1)
                P.on("pe", pe_tr)
                s_pe.n += 1
                tr_mark[i] = s_pe.n

            def emit_av(i):
                kb, j, w = updates[i]

                def pe_av(t, i=i, kb=kb):
                    t.wait_ge(s_v2[vld[kb][0]].h, vld[kb][1])
                    t.wait_ge(s_act.h, ptv_mark[i])
                    if i >= 1:
                        t.wait_ge(s_dve.h, oup_mark[i - 1])
                    n = 0
                    for kc in range(4):
                        for dq in range(4):
                            n += 1
                            mm = t.matmul(ps_c[:, dq, :],
                                          pt_sb[i % 2][:, kc, :],
                                          v_blk[kb % 2][:, kc, dq * 512:(dq + 1) * 512],
                                          start=(kc == 0), stop=(kc == 3),
                                          skip_group_check=True)
                            if n == 8:
                                mm.then_inc(s_junk.h, 1)
                    mm.then_inc(s_pe.h, 1)
                P.on("pe", pe_av)
                s_junk.n += 1
                s_pe.n += 1
                av_mark[i] = s_pe.n
                vld_guard[kb] = s_pe.n

            def emit_bm(i, need_sep):
                kb, j, w = updates[i]
                if w is not None:

                    def dve_mask(v, i=i, j=j, w=w, sj=s_dve.n):
                        v.wait_ge(s_dve.h, sj)
                        v.wait_ge(s_pe.h, sc_mark[i])
                        v.wait_ge(s_mk2[mask_loaded[j][0]].h, mask_loaded[j][1])
                        v.tensor_tensor(ps_a[:, i % 2, :], ps_a[:, i % 2, :],
                                        mask_sb[j % 2][:, w, :],
                                        ALU.add).then_inc(s_dve.h, 1)
                    P.on("dve", dve_mask)
                    s_dve.n += 1
                    msk_mark[i] = s_dve.n
                    mask_guard[j] = s_dve.n

                def dve_bm(v, i=i, j=j, sj=s_dve.n):
                    v.wait_ge(s_dve.h, sj)
                    v.wait_ge(s_pe.h, sc_mark[i])
                    v.reduce_max(stats[:, ST_BM + j:ST_BM + j + 1],
                                 ps_a[:, i % 2, :], axis=AX.X).then_inc(s_dve.h, 1)
                P.on("dve", dve_bm)
                s_dve.n += 1

            def emit_lm(i):
                kb, j, w = updates[i]

                def dve_lm(v, i=i, j=j, sj=s_dve.n):
                    v.wait_ge(s_dve.h, sj)
                    v.wait_ge(s_act.h, exp_mark[i])
                    v.tensor_copy(stats[:, ST_CSV + j:ST_CSV + j + 1],
                                  stats[:, ST_CORR + j:ST_CORR + j + 1])
                    v.drain()
                    v.scalar_tensor_tensor(
                        stats[:, ST_L + j:ST_L + j + 1],
                        stats[:, ST_L + j:ST_L + j + 1],
                        stats[:, ST_CORR + j:ST_CORR + j + 1],
                        stats[:, ST_BS + j:ST_BS + j + 1],
                        op0=ALU.mult, op1=ALU.add)
                    v.drain()
                    v.tensor_scalar_mul(stats[:, ST_M + j:ST_M + j + 1],
                                        stats[:, ST_NM + j:ST_NM + j + 1],
                                        -1.0).then_inc(s_dve.h, 1)
                P.on("dve", dve_lm)
                s_dve.n += 1

            def emit_nm(i, sep):
                kb, j, w = updates[i]

                def dve_nm(v, i=i, j=j, sep=sep, sj=s_dve.n):
                    v.wait_ge(s_dve.h, sj)
                    v.tensor_scalar(
                        stats[:, ST_NM + j:ST_NM + j + 1],
                        stats[:, ST_BM + j:ST_BM + j + 1],
                        stats[:, ST_M + j:ST_M + j + 1], -1.0,
                        op0=ALU.max, op1=ALU.mult,
                    ).then_inc(s_dve.h, 1)
                P.on("dve", dve_nm)
                s_dve.n += 1
                nm_mark[i] = s_dve.n

            def emit_o(i):
                kb, j, w = updates[i]

                def dve_o(v, i=i, j=j, sj=s_dve.n):
                    v.wait_ge(s_dve.h, sj)
                    v.wait_ge(s_pe.h, av_mark[i])
                    v.scalar_tensor_tensor(
                        o_acc[:, j, :], o_acc[:, j, :],
                        stats[:, ST_CSV + j:ST_CSV + j + 1],
                        ps_c[:, :, :].rearrange("pp a b -> pp (a b)"),
                        op0=ALU.mult, op1=ALU.add,
                    ).then_inc(s_dve.h, 1)
                P.on("dve", dve_o)
                s_dve.n += 1
                oup_mark[i] = s_dve.n

            def emit_act_group(i):
                kb, j, w = updates[i]
                if i >= 1:
                    def act_ptv(s, i=i):
                        s.wait_ge(s_pe.h, tr_mark[i - 1])
                        s.copy(
                            pt_sb[(i - 1) % 2][:, :, :],
                            ps_t[(i - 1) % 2][:, :512].rearrange(
                                "pp (a b) -> pp a b", a=4),
                        ).then_inc(s_act.h, 1)
                    P.on("act", act_ptv)
                    s_act.n += 1
                    ptv_mark[i - 1] = s_act.n

                def act_corr(s, i=i, j=j):
                    s.wait_ge(s_dve.h, nm_mark[i])
                    s.activation(stats[:, ST_CORR + j:ST_CORR + j + 1],
                                 stats[:, ST_M + j:ST_M + j + 1], AF.Exp,
                                 bias=stats[:, ST_NM + j:ST_NM + j + 1],
                                 scale=1.0).then_inc(s_act.h, 1)
                P.on("act", act_corr)
                s_act.n += 1

                def act_exp(s, i=i, j=j, w=w):
                    if w is not None:
                        s.wait_ge(s_dve.h, msk_mark[i])
                    s.activation(p_sb[i % 2][:, :], ps_a[:, i % 2, :], AF.Exp,
                                 bias=stats[:, ST_NM + j:ST_NM + j + 1], scale=1.0,
                                 accum_out=stats[:, ST_BS + j:ST_BS + j + 1],
                                 ).then_inc(s_act.h, 1)
                P.on("act", act_exp)
                s_act.n += 1
                exp_mark[i] = s_act.n

            def run_epi_step():
                if not epi_queue:
                    return
                j, step = epi_queue[0]
                epi_queue[0][1] += 1
                if step == 0:
                    def dve_e0(v, j=j, sj=s_dve.n):
                        v.wait_ge(s_dve.h, sj)
                        v.wait_ge(s_dve.h, oup_last[j])
                        v.reciprocal(stats[:, ST_RECIP:ST_RECIP + 1],
                                     stats[:, ST_L + j:ST_L + j + 1]
                                     ).then_inc(s_dve.h, 1)
                    P.on("dve", dve_e0)
                    s_dve.n += 1
                elif step == 1:
                    def dve_e1(v, j=j, sj=s_dve.n):
                        v.wait_ge(s_dve.h, sj)
                        v.wait_ge(s_xq.h, xq_loaded[j])
                        v.scalar_tensor_tensor(
                            o_acc[:, j, :], o_acc[:, j, :],
                            stats[:, ST_RECIP:ST_RECIP + 1], xq_t[:, :],
                            op0=ALU.mult, op1=ALU.add,
                        ).then_inc(s_dve.h, 1)
                    P.on("dve", dve_e1)
                    s_dve.n += 1
                    xq_guard.append(s_dve.n)
                elif step == 2:
                    def dve_e2(v, j=j, sj=s_dve.n):
                        v.wait_ge(s_dve.h, sj)
                        for k in range(4):
                            v.bn_stats(bnbuf[:, 2, k * 6:(k + 1) * 6],
                                       o_acc[:, j, k * 512:(k + 1) * 512])
                        v.drain()
                        v.sem_inc(s_dve.h, 1)
                    P.on("dve", dve_e2)
                    s_dve.n += 1
                elif step == 3:
                    def dve_e3(v, j=j, sj=s_dve.n):
                        v.wait_ge(s_dve.h, sj)
                        v.bn_aggr(stats[:, ST_MEAN:ST_MEAN + 2], bnbuf[:, 2, :])
                        v.drain()
                        v.sem_inc(s_dve.h, 1)
                    P.on("dve", dve_e3)
                    s_dve.n += 1
                    vm = s_dve.n

                    def act_e3(s, j=j, vm=vm):
                        s.wait_ge(s_dve.h, vm)
                        s.activation(stats[:, ST_LNV:ST_LNV + 1],
                                     stats[:, ST_VAR:ST_VAR + 1], AF.Ln,
                                     bias=stats[:, ST_EPS:ST_EPS + 1], scale=1.0)
                        s.drain()
                        s.activation(stats[:, ST_RSTD:ST_RSTD + 1],
                                     stats[:, ST_LNV:ST_LNV + 1], AF.Exp,
                                     bias=0.0, scale=-0.5).then_inc(s_act.h, 1)
                    P.on("act", act_e3)
                    s_act.n += 1
                    rstd_ready[j] = s_act.n
                elif step == 4:
                    hw = h1_written[-1]

                    def dve_e4(v, j=j, hw=hw, sj=s_dve.n):
                        v.wait_ge(s_dve.h, sj)
                        if hw:
                            v.wait_ge(s_sh1.h, hw)
                        v.wait_ge(s_act.h, rstd_ready[j])
                        v.tensor_scalar(
                            h1st[:, :], o_acc[:, j, :],
                            stats[:, ST_MEAN:ST_MEAN + 1],
                            stats[:, ST_RSTD:ST_RSTD + 1],
                            op0=ALU.subtract, op1=ALU.mult,
                        ).then_inc(s_dve.h, 1)
                    P.on("dve", dve_e4)
                    s_dve.n += 1
                    h1_ready[j] = s_dve.n
                elif step == 5:
                    def sp_h1(g, j=j):
                        g.wait_ge(s_dve.h, h1_ready[j])
                        g.dma_start(
                            out=h1_dram[j * 128:(j + 1) * 128, :], in_=h1st[:, :],
                        ).then_inc(s_sh1.h, 16)
                    P.on("pool", sp_h1)
                    s_sh1.n += 16
                    h1_written.append(s_sh1.n)
                    epi_queue.pop(0)

            for i0 in (0, 1):
                kb0, j0, w0 = updates[i0]
                if w0 == 0 and j0 not in mask_loaded:
                    emit_mask_load(j0)
            emit_scores(0)
            if NU > 1:
                emit_scores(1)
            emit_bm(0, need_sep=True)

            for i in range(NU):
                kb, j, w = updates[i]
                if i >= 1:
                    emit_lm(i - 1)
                emit_nm(i, sep=True)
                if i + 1 < NU:
                    emit_bm(i + 1, need_sep=False)
                if i >= 1:
                    emit_o(i - 1)
                emit_act_group(i)
                run_epi_step()
                emit_tr(i)
                if i >= 1:
                    emit_av(i - 1)
                if i + 2 < NU:
                    kb2, j2, w2 = updates[i + 2]
                    if w2 == 0 and j2 not in mask_loaded:
                        emit_mask_load(j2)
                    emit_scores(i + 2)
                if kb == ret[j]:
                    emit_xq_load(j)
                    epi_queue.append([j, 0])

            iN = NU - 1

            def act_ptv_tail(s, iN=iN):
                s.wait_ge(s_pe.h, tr_mark[iN])
                s.copy(
                    pt_sb[iN % 2][:, :, :],
                    ps_t[iN % 2][:, :512].rearrange("pp (a b) -> pp a b", a=4),
                ).then_inc(s_act.h, 1)
            P.on("act", act_ptv_tail)
            s_act.n += 1
            ptv_mark[iN] = s_act.n

            emit_av(iN)
            emit_lm(iN)
            emit_o(iN)

            for i2 in range(NU):
                kb2, j2, _ = updates[i2]
                if kb2 == ret[j2]:
                    oup_last[j2] = oup_mark[i2]

            while epi_queue:
                run_epi_step()

            if debug:
                def dbg_oa(sp, p=p, dm=s_dve.n):
                    sp.wait_ge(s_dve.h, dm)
                    sp.dma_start(out=dbg["oacc"][p, :, :, :], in_=o_acc[:, :, :]).then_inc(s_sout.h, 16)
                P.on("sp", dbg_oa)
                s_sout.n += 16

                def dbg_h1(sp, p=p, hm=s_sh1.n):
                    sp.wait_ge(s_sh1.h, hm)
                    sp.dma_start(out=dbg["h1"][p, :, :], in_=h1_dram.ap()).then_inc(s_sout.h, 16)
                P.on("sp", dbg_h1)
                s_sout.n += 16

            barrier()

            if not want(f"ffn{p}"):
                return False

            H1_ALL = s_sh1.n

            h1bf_guard = {}
            trv = []
            for tt in range(8):
                g2 = h1bf_guard.get(tt - 2, 0)

                def pool_h1(g, tt=tt, g2=g2, h1a=H1_ALL):
                    g.wait_ge(s_sh1.h, h1a)
                    if g2:
                        g.wait_ge(s_pe.h, g2)
                    g.dma_start(
                        out=gu_blk[tt % 2][:, :, :, :].rearrange(
                            "pp a b c -> pp (a b c)")[:, 0:2048],
                        in_=h1_dram[tt * 128:(tt + 1) * 128, :],
                    ).then_inc(s_h1t2[tt % 2].h, 16)
                P.on("pool", pool_h1)
                s_h1t2[tt % 2].n += 16
                ldm = (tt % 2, s_h1t2[tt % 2].n)
                base = s_pe.n

                def pe_h1t(t, tt=tt, ldm=ldm, base=base):
                    t.wait_ge(s_h1t2[ldm[0]].h, ldm[1])
                    src = gu_blk[tt % 2][:, :, :, :].rearrange(
                        "pp a b c -> pp (a b c)")[:, 0:2048]
                    for hc in range(16):
                        gidx = tt * 16 + hc
                        if gidx >= 2:
                            t.wait_ge(s_act.h, trv[gidx - 2])
                        tr = t.transpose(ps_t[hc % 2][:, 0:128],
                                         src[:, hc * 128:(hc + 1) * 128],
                                         ident[:, :])
                        tr.then_inc(s_pe.h, 1)
                P.on("pe", pe_h1t)
                s_pe.n += 16
                h1bf_guard[tt] = s_pe.n

                for hc in range(16):
                    def act_h1t(s, tt=tt, hc=hc, pm=base + hc + 1):
                        s.wait_ge(s_pe.h, pm)
                        s.copy(h1T[:, hc, tt * 128:(tt + 1) * 128],
                               ps_t[hc % 2][:, 0:128]).then_inc(s_act.h, 1)
                    P.on("act", act_h1t)
                    s_act.n += 1
                    trv.append(s_act.n)
            H1T_ACT = s_act.n

            gu_ld = {}
            gu_pe = {}
            silu_m = {}
            u1_m = {}
            hdn_m = {}
            for j in range(44):
                g2 = gu_pe.get(j - 2, 0) if j >= 2 else h1bf_guard[6 + j]

                def pool_gu(g, j=j, g2=g2):
                    if g2:
                        g.wait_ge(s_pe.h, g2)
                    g.dma_start(out=gu_blk[j % 2][:, 0, :, :],
                                in_=d_gateT[j, :, :, :]).then_inc(s_gu2[j % 2].h, 16)
                    g.dma_start(out=gu_blk[j % 2][:, 1, :, :],
                                in_=d_upT[j, :, :, :]).then_inc(s_gu2[j % 2].h, 16)
                P.on("pool", pool_gu)
                s_gu2[j % 2].n += 32
                gu_ld[j] = (j % 2, s_gu2[j % 2].n)

                sg = silu_m.get(j - 1, 0)
                ug = u1_m.get(j - 1, 0)

                def pe_gu(t, j=j, ldm=gu_ld[j], sg=sg, ug=ug, h1a=H1T_ACT):
                    t.wait_ge(s_gu2[ldm[0]].h, ldm[1])
                    if j == 0:
                        t.wait_ge(s_act.h, h1a)
                    if sg:
                        t.wait_ge(s_act.h, sg)
                    if ug:
                        t.wait_ge(s_dve.h, ug)
                    n = 0
                    for which, ps in ((0, ps_a), (1, ps_c)):
                        for hc in range(16):
                            for half in range(2):
                                n += 1
                                mm = t.matmul(
                                    ps[:, half, :],
                                    gu_blk[j % 2][:, which, hc, :],
                                    h1T[:, hc, half * 512:(half + 1) * 512],
                                    start=(hc == 0), stop=(hc == 15),
                                    skip_group_check=True)
                                if n % 16 == 0 and n != 64:
                                    mm.then_inc(s_junk.h, 1)
                    mm.then_inc(s_pe.h, 1)
                P.on("pe", pe_gu)
                s_junk.n += 3
                s_pe.n += 1
                gu_pe[j] = s_pe.n
                pm = s_pe.n

                hg2 = hdn_m.get(j - 2, 0)

                def act_silu(s, j=j, pm=pm, hg2=hg2):
                    s.wait_ge(s_pe.h, pm)
                    if hg2:
                        s.wait_ge(s_dve.h, hg2)
                    s.activation(silu_sb[j % 2][:, :],
                                 ps_a[:, 0:2, :].rearrange("pp a b -> pp (a b)"),
                                 AF.Silu, scale=gs_sb[:, j:j + 1],
                                 ).then_inc(s_act.h, 1)
                P.on("act", act_silu)
                s_act.n += 1
                silu_m[j] = s_act.n

                def dve_u1(v, j=j, pm=pm, sj=s_dve.n):
                    v.wait_ge(s_dve.h, sj)
                    v.wait_ge(s_pe.h, pm)
                    v.tensor_scalar_mul(
                        u1_sb[j % 2][:, :],
                        ps_c[:, 0:2, :].rearrange("pp a b -> pp (a b)"),
                        us_sb[:, j:j + 1]).then_inc(s_dve.h, 1)
                P.on("dve", dve_u1)
                s_dve.n += 1
                u1_m[j] = s_dve.n

                if j >= 1:
                    jd = j - 1

                    def dve_hdn(v, jd=jd, sj=s_dve.n):
                        v.wait_ge(s_dve.h, sj)
                        v.wait_ge(s_act.h, silu_m[jd])
                        tgt = (hdn_lo[:, jd, :] if jd < 32
                               else hdn_hi[:, jd - 32, :])
                        v.tensor_tensor(tgt, silu_sb[jd % 2][:, :],
                                        u1_sb[jd % 2][:, :],
                                        ALU.mult).then_inc(s_dve.h, 1)
                    P.on("dve", dve_hdn)
                    s_dve.n += 1
                    hdn_m[jd] = s_dve.n

            def dve_hdn_last(v, sj=s_dve.n):
                v.wait_ge(s_dve.h, sj)
                v.wait_ge(s_act.h, silu_m[43])
                v.tensor_tensor(hdn_hi[:, 11, :], silu_sb[43 % 2][:, :],
                                u1_sb[43 % 2][:, :], ALU.mult).then_inc(s_dve.h, 1)
            P.on("dve", dve_hdn_last)
            s_dve.n += 1
            hdn_m[43] = s_dve.n
            HDN_DONE = s_dve.n

            def pool_ds(g, hd=HDN_DONE):
                g.wait_ge(s_dve.h, hd)
                g.dma_start(out=ds_b[:, :], in_=d_ds[:, :]).then_inc(s_ds.h, 16)
            P.on("pool", pool_ds)
            s_ds.n += 16
            DS_READY = s_ds.n

            def hdn_sl(jj, tt):
                return (hdn_lo[:, jj, tt * 128:(tt + 1) * 128] if jj < 32
                        else hdn_hi[:, jj - 32, tt * 128:(tt + 1) * 128])

            dn_pe = {}
            dn_ld = {}
            x2p_m = {}
            x2_done_tt = {}
            x2buf_guard = [0, 0]
            h1sl_guard = [0, 0]
            pairrot = 0
            for qp in range(2):
                for gidx in range(9):
                    glen = DOWN_JG[gidx]
                    j0 = sum(DOWN_JG[:gidx])
                    prevkey = (qp, gidx - 2) if gidx >= 2 else (
                        (qp - 1, 8) if gidx == 0 else (qp - 1, 7))
                    g2 = dn_pe.get(prevkey, 0)

                    def pool_dn(g, qp=qp, gidx=gidx, j0=j0, glen=glen, g2=g2,
                                hd=HDN_DONE):
                        if g2:
                            g.wait_ge(s_pe.h, g2)
                        if qp == 0 and gidx == 0:
                            g.wait_ge(s_dve.h, hd)
                        g.dma_start(
                            out=down_bf[gidx % 2][:, 0:glen, :],
                            in_=d_downT[qp, j0:j0 + glen, :, :].rearrange(
                                "jj pp d -> pp jj d"),
                        ).then_inc(s_dn2[gidx % 2].h, 16)
                    P.on("pool", pool_dn)
                    s_dn2[gidx % 2].n += 16
                    dn_ld[(qp, gidx)] = (gidx % 2, s_dn2[gidx % 2].n)

                    for tt in range(8):
                        bp = pairrot % 3
                        pairrot += 1
                        evg = x2p_m.get((qp, gidx, tt - 3), 0) if tt >= 3 else (
                            x2p_m.get((qp, gidx - 1, 5 + tt), 0) if gidx >= 1 else
                            x2p_m.get((qp - 1, 8, 5 + tt), 0))

                        def pe_dn(t, qp=qp, gidx=gidx, glen=glen, j0=j0, tt=tt,
                                  bp=bp, ldm=dn_ld[(qp, gidx)], evg=evg,
                                  hd=HDN_DONE):
                            t.wait_ge(s_dn2[ldm[0]].h, ldm[1])
                            t.wait_ge(s_dve.h, hd)
                            if evg:
                                t.wait_ge(s_dve.h, evg)
                            n = 0
                            for jl in range(glen):
                                for qh in range(2):
                                    n += 1
                                    ps_dn = (ps_c[:, qh, :] if bp == 0 else
                                             ps_c[:, 2 + qh, :] if bp == 1 else
                                             ps_a[:, qh, :])
                                    mm = t.matmul(
                                        ps_dn,
                                        hdn_sl(j0 + jl, tt),
                                        down_bf[gidx % 2][:, jl,
                                                          qh * 512:(qh + 1) * 512],
                                        start=(jl == 0), stop=(jl == glen - 1),
                                        skip_group_check=True)
                                    if n % 10 == 0 and n != glen * 2:
                                        mm.then_inc(s_junk.h, 1)
                            mm.then_inc(s_pe.h, 1)
                        P.on("pe", pe_dn)
                        s_junk.n += (glen * 2 - 1) // 10
                        s_pe.n += 1
                        dn_pe[(qp, gidx)] = s_pe.n
                        pm = s_pe.n

                        def dve_x2p(v, qp=qp, gidx=gidx, tt=tt, bp=bp, pm=pm,
                                    sj=s_dve.n):
                            v.wait_ge(s_dve.h, sj)
                            v.wait_ge(s_pe.h, pm)
                            src = (ps_c[:, 0:2, :] if bp == 0 else
                                   ps_c[:, 2:4, :] if bp == 1 else
                                   ps_a[:, 0:2, :]).rearrange(
                                "pp a b -> pp (a b)")
                            if gidx == 0:
                                v.tensor_copy(x2part[:, tt, :], src)
                            else:
                                v.tensor_tensor(x2part[:, tt, :],
                                                x2part[:, tt, :], src, ALU.add)
                            v.drain()
                            v.sem_inc(s_dve.h, 1)
                        P.on("dve", dve_x2p)
                        s_dve.n += 1
                        x2p_m[(qp, gidx, tt)] = s_dve.n

                for tt in range(8):
                    hg = h1sl_guard[tt % 2]

                    def sp_h1sl(sp, qp=qp, tt=tt, hg=hg, h1a=H1_ALL):
                        sp.wait_ge(s_sh1.h, h1a)
                        if hg:
                            sp.wait_ge(s_dve.h, hg)
                        sp.dma_start(
                            out=h1sl2[tt % 2][:, :],
                            in_=h1_dram[tt * 128:(tt + 1) * 128,
                                        qp * 1024:(qp + 1) * 1024],
                        ).then_inc(s_h1s2[tt % 2].h, 16)
                    P.on("sp", sp_h1sl)
                    s_h1s2[tt % 2].n += 16
                    h1m = s_h1s2[tt % 2].n
                    accg = x2p_m[(qp, 8, tt)]

                    par = tt % 2
                    xg = x2buf_guard[par]

                    def dve_x2(v, qp=qp, tt=tt, h1m=h1m, par=par,
                               accg=accg, xg=xg, dsr=DS_READY, sj=s_dve.n):
                        v.wait_ge(s_dve.h, sj)
                        v.wait_ge(s_h1s2[par].h, h1m)
                        v.wait_ge(s_dve.h, accg)
                        v.wait_ge(s_ds.h, dsr)
                        if xg:
                            v.wait_ge(s_sx2b[par].h, xg)
                        for qh in range(2):
                            q = 2 * qp + qh
                            v.tensor_tensor(
                                x2buf[par][:, qh * 512:(qh + 1) * 512],
                                x2part[:, tt, qh * 512:(qh + 1) * 512],
                                ds_b[:, q * 512:(q + 1) * 512], ALU.mult)
                        v.drain()
                        v.tensor_tensor(x2buf[par][:, :], x2buf[par][:, :],
                                        h1sl2[par][:, :], ALU.add).then_inc(s_dve.h, 1)
                    P.on("dve", dve_x2)
                    s_dve.n += 1
                    xm = s_dve.n

                    def sp_x2(sp, tt=tt, qp=qp, xm=xm, par=par):
                        sp.wait_ge(s_dve.h, xm)
                        sp.dma_start(
                            out=x2_dram[tt * 128:(tt + 1) * 128,
                                        qp * 1024:(qp + 1) * 1024],
                            in_=x2buf[par][:, :],
                        ).then_inc(s_sx2b[par].h, 16)
                    P.on("act", sp_x2)
                    s_sx2b[par].n += 16
                    x2buf_guard[par] = s_sx2b[par].n
                    h1sl_guard[tt % 2] = s_dve.n
                    if qp == 1:
                        x2_done_tt[tt] = (par, s_sx2b[par].n)

            norm_guard = [0, 0]
            out_prev = [0]
            for tt in range(8):
                par = tt % 2
                ng = norm_guard[par]

                def sp_x2ld(sp, tt=tt, par=par, xdt=x2_done_tt[tt], ng=ng):
                    sp.wait_ge(s_sx2b[xdt[0]].h, xdt[1])
                    if ng:
                        sp.wait_ge(s_dve.h, ng)
                    sp.dma_start(
                        out=x2ld2[par][:, :],
                        in_=x2_dram[tt * 128:(tt + 1) * 128, :],
                    ).then_inc(s_x2l2[par].h, 16)
                P.on("sp", sp_x2ld)
                s_x2l2[par].n += 16
                xm = s_x2l2[par].n

                def dve_ln2(v, tt=tt, par=par, xm=xm,
                            sj=s_dve.n, b=ST2_BASE[par]):
                    v.wait_ge(s_dve.h, sj)
                    v.wait_ge(s_x2l2[par].h, xm)
                    for k in range(4):
                        v.bn_stats(bnbuf[:, par, k * 6:(k + 1) * 6],
                                   x2ld2[par][:, k * 512:(k + 1) * 512])
                    v.drain()
                    v.bn_aggr(stats[:, b + ST2_MEAN:b + ST2_MEAN + 2],
                              bnbuf[:, par, :])
                    v.drain()
                    v.sem_inc(s_dve.h, 1)
                P.on("dve", dve_ln2)
                s_dve.n += 1
                vm = s_dve.n

                def act_ln2(s, vm=vm, b=ST2_BASE[par]):
                    s.wait_ge(s_dve.h, vm)
                    s.activation(stats[:, b + ST2_LNV:b + ST2_LNV + 1],
                                 stats[:, b + ST2_VAR:b + ST2_VAR + 1], AF.Ln,
                                 bias=stats[:, ST_EPS:ST_EPS + 1], scale=1.0)
                    s.drain()
                    s.activation(stats[:, b + ST2_RSTD:b + ST2_RSTD + 1],
                                 stats[:, b + ST2_LNV:b + ST2_LNV + 1], AF.Exp,
                                 bias=0.0, scale=-0.5).then_inc(s_act.h, 1)
                P.on("act", act_ln2)
                s_act.n += 1
                am = s_act.n
                og = out_prev[-1]

                def dve_norm(v, par=par, am=am, og=og, sj=s_dve.n,
                             b=ST2_BASE[par]):
                    v.wait_ge(s_dve.h, sj)
                    v.wait_ge(s_act.h, am)
                    if og:
                        v.wait_ge(s_sout.h, og)
                    v.tensor_scalar(
                        outst[:, :], x2ld2[par][:, :],
                        stats[:, b + ST2_MEAN:b + ST2_MEAN + 1],
                        stats[:, b + ST2_RSTD:b + ST2_RSTD + 1],
                        op0=ALU.subtract, op1=ALU.mult).then_inc(s_dve.h, 1)
                P.on("dve", dve_norm)
                s_dve.n += 1
                nm2 = s_dve.n
                norm_guard[par] = nm2

                def sp_out(sp, tt=tt, nm2=nm2, p=p):
                    sp.wait_ge(s_dve.h, nm2)
                    row0 = (p * 8 + tt) * 128
                    sp.dma_start(
                        out=d_out[row0:row0 + 128, :], in_=outst[:, :],
                    ).then_inc(s_sout.h, 16)
                P.on("act", sp_out)
                s_sout.n += 16
                out_prev.append(s_sout.n)

            barrier()
            return True

        for p in range(2):
            if not want(f"qproj{p}"):
                break
            if not emit_pass(p):
                break

        _emit_final_and_block()
    return nc


def kernel(**inputs):
    in_maps, row_maps = _stage(inputs)
    nc = build_nc(debug=False)
    res = run_bass_kernel_spmd(nc, in_maps, core_ids=list(range(NCORES)))
    out = np.empty((S, E), np.float32)
    for c in range(NCORES):
        out[row_maps[c]] = res.results[c]["out"]
    return out.reshape(1, S, E)


# version 11
# speedup vs baseline: 1.0566x; 1.0566x over previous
import numpy as np
import ml_dtypes
import concourse.bass as bass
import concourse.mybir as mybir
from concourse.bass_utils import run_bass_kernel_spmd

F32 = mybir.dt.float32
BF16 = mybir.dt.bfloat16
AF = mybir.ActivationFunctionType
ALU = mybir.AluOpType
AX = mybir.AxisListType

E = 2048
S = 16384
NCORES = 8
SCALE = 1.0 / float(np.sqrt(2048.0))
EPS = 1e-5
NEG = -1.0e9
RET1 = [1, 3, 5, 7, 9, 11, 13, 15]
DOWN_JG = [5, 5, 5, 5, 5, 5, 5, 5, 4]

ST_M = 0
ST_L = 8
ST_BM = 16
ST_NM = 24
ST_CORR = 32
ST_BS = 40
ST_RECIP = 48
ST_MEAN = 49
ST_VAR = 50
ST_LNV = 51
ST_RSTD = 52
ST2_MEAN = 0
ST2_VAR = 1
ST2_LNV = 2
ST2_RSTD = 3
ST2_BASE = (56, 73)
ST_CSV = 64
ST_EPS = 72


def _pat(c):
    return [c, 15 - c, 16 + c, 31 - c, 32 + c, 47 - c, 48 + c, 63 - c]


def _stage(inputs):
    x = np.asarray(inputs["L0_attn_x"], np.float32).reshape(S, E)
    wq = np.asarray(inputs["L0_attn_Wq"], np.float32)
    wk = np.asarray(inputs["L0_attn_Wk"], np.float32)
    wv = np.asarray(inputs["L0_attn_Wv"], np.float32)
    gw = np.asarray(inputs["L0_ffn_gate_w"], np.float32)
    uw = np.asarray(inputs["L0_ffn_up_w"], np.float32)
    dw = np.asarray(inputs["L0_ffn_down_w"], np.float32)
    gs = np.asarray(inputs["L0_ffn_gate_s"], np.float32)
    us = np.asarray(inputs["L0_ffn_up_s"], np.float32)
    ds = np.asarray(inputs["L0_ffn_down_s"], np.float32)

    C = np.ascontiguousarray
    shared = {
        "wqT": C(wq.reshape(16, 128, 16, 128).transpose(0, 3, 2, 1)),
        "wkT": C(wk.reshape(16, 128, 16, 128).transpose(0, 3, 2, 1)),
        "wvT": C(wv.reshape(E, 16, 128).transpose(1, 2, 0)),
        "gateT": C(gw.reshape(44, 128, 16, 128).transpose(0, 3, 2, 1)),
        "upT": C(uw.reshape(44, 128, 16, 128).transpose(0, 3, 2, 1)),
        "downT": C(dw.reshape(2, 1024, 44, 128).transpose(0, 2, 3, 1)),
        "gate_s": C(gs.reshape(44, 128).T),
        "up_s": C(us.reshape(44, 128).T),
        "down_s": C(np.broadcast_to(ds.reshape(1, E), (128, E))),
        "ident": np.eye(128, dtype=ml_dtypes.bfloat16),
    }
    tri = np.triu(np.ones((128, 128), np.float32), k=1) * NEG

    in_maps, row_maps = [], []
    for c in range(NCORES):
        tiles = [[64 * p + t for t in _pat(c)] for p in (0, 1)]
        rows = np.concatenate([
            np.arange(t * 128, t * 128 + 128) for p in (0, 1) for t in tiles[p]
        ])
        row_maps.append(rows)
        xq = C(x[rows])
        xqT = C(
            x[rows].reshape(2, 8, 128, 16, 128).transpose(0, 4, 3, 1, 2)
        ).reshape(2, 128, 16, 1024)
        kvrows = np.concatenate([
            np.arange(4096 * m + 512 * c, 4096 * m + 512 * c + 512)
            for m in range(4)
        ])
        xkvT = C(x[kvrows].reshape(4, 512, 16, 128).transpose(3, 2, 0, 1))
        masks = np.zeros((2, 8, 2, 128, 512), np.float32)
        for p in (0, 1):
            for j in range(8):
                t = tiles[p][j]
                ret = RET1[j] + 16 * p
                for which, kb in enumerate((ret - 1, ret)):
                    for sub in range(4):
                        kt = 4 * kb + sub
                        blk = masks[p, j, which, :, sub * 128:(sub + 1) * 128]
                        if kt == t:
                            blk[:] = tri
                        elif kt > t:
                            blk[:] = NEG
        m = dict(shared)
        m.update({"xq": xq, "xqT": xqT, "xkvT": xkvT, "masks": C(masks)})
        in_maps.append(m)
    return in_maps, row_maps


class Ctr:
    def __init__(self, handle):
        self.h = handle
        self.n = 0


class Prog:
    ENGS = ("pe", "act", "dve", "pool", "sp")

    def __init__(self):
        self.ops = {e: [] for e in self.ENGS}

    def on(self, eng, fn):
        self.ops[eng].append(fn)


def build_nc(debug=False, stop_after=None, flash_trunc=0):
    _ph_order = ["init", "kvk", "kv", "qproj0", "flash0", "ffn0",
                 "qproj1", "flash1", "ffn1"]
    _stop_idx = _ph_order.index(stop_after) if stop_after else 99

    def want(ph):
        return _ph_order.index(ph) <= _stop_idx

    nc = bass.Bass("TRN2", target_bir_lowering=False, debug=False)

    d_xq = nc.dram_tensor("xq", [2048, E], F32, kind="ExternalInput")
    d_xqT = nc.dram_tensor("xqT", [2, 128, 16, 1024], F32, kind="ExternalInput")
    d_xkvT = nc.dram_tensor("xkvT", [128, 16, 4, 512], F32, kind="ExternalInput")
    d_wqT = nc.dram_tensor("wqT", [16, 128, 16, 128], F32, kind="ExternalInput")
    d_wkT = nc.dram_tensor("wkT", [16, 128, 16, 128], F32, kind="ExternalInput")
    d_wvT = nc.dram_tensor("wvT", [16, 128, E], F32, kind="ExternalInput")
    d_gateT = nc.dram_tensor("gateT", [44, 128, 16, 128], F32, kind="ExternalInput")
    d_upT = nc.dram_tensor("upT", [44, 128, 16, 128], F32, kind="ExternalInput")
    d_downT = nc.dram_tensor("downT", [2, 44, 128, 1024], F32, kind="ExternalInput")
    d_gs = nc.dram_tensor("gate_s", [128, 44], F32, kind="ExternalInput")
    d_us = nc.dram_tensor("up_s", [128, 44], F32, kind="ExternalInput")
    d_ds = nc.dram_tensor("down_s", [128, E], F32, kind="ExternalInput")
    d_ident = nc.dram_tensor("ident", [128, 128], BF16, kind="ExternalInput")
    d_masks = nc.dram_tensor("masks", [2, 8, 2, 128, 512], F32, kind="ExternalInput")
    d_out = nc.dram_tensor("out", [2048, E], F32, kind="ExternalOutput")

    kg_in = [nc.dram_tensor(f"kgin{m}", [E, 512], BF16) for m in range(4)]
    vg_in = [nc.dram_tensor(f"vgin{m}", [512, E], BF16) for m in range(4)]
    kg_out = [nc.dram_tensor(f"kgout{m}", [8 * E, 512], BF16) for m in range(4)]
    vg_out = [nc.dram_tensor(f"vgout{m}", [8 * 512, E], BF16) for m in range(4)]
    h1_dram = nc.dram_tensor("h1d", [8 * 128, E], F32)
    x2_dram = nc.dram_tensor("x2d", [8 * 128, E], F32)

    dbg = {}
    if debug:
        dbg["kg0"] = nc.dram_tensor("dbg_kg0", [8 * E, 512], BF16, kind="ExternalOutput")
        dbg["vg0"] = nc.dram_tensor("dbg_vg0", [8 * 512, E], BF16, kind="ExternalOutput")
        dbg["q"] = nc.dram_tensor("dbg_q", [2, 128, 16, 1024], BF16, kind="ExternalOutput")
        dbg["h1"] = nc.dram_tensor("dbg_h1", [2, 8 * 128, E], F32, kind="ExternalOutput")
        dbg["oacc"] = nc.dram_tensor("dbg_oacc", [2, 128, 8, E], F32, kind="ExternalOutput")

    from contextlib import ExitStack
    with ExitStack() as es:
        BASE = (nc.sbuf_base + 31) // 32 * 32
        LIMIT = nc.SBUF_PARTITION_SIZE_BYTES

        def sb(name, off_kb, shape, dt):
            off = BASE + int(off_kb * 1024)
            sz = int(np.prod(shape[1:])) * mybir.dt.size(dt)
            assert off + sz <= LIMIT, (name, off, sz, LIMIT)
            return nc.alloc_sbuf_tensor_at(name, shape, dt, offset=off)

        ident = sb("identt", 0.0, [128, 128], BF16)
        stats = sb("stats", 256 / 1024.0, [128, 80], F32)
        gs_sb = sb("gs_sb", 576 / 1024.0, [128, 44], F32)
        us_sb = sb("us_sb", 768 / 1024.0, [128, 44], F32)
        xkvT_bf = sb("xkvT_bf", 1, [128, 16, 4, 512], BF16)
        o_acc = sb("o_acc", 1, [128, 8, E], F32)
        hdn_lo = sb("hdn_lo", 1, [128, 32, 1024], BF16)
        wk_sb = sb("wk_sb", 65, [128, 16, 16, 128], BF16)
        wv_sb = sb("wv_sb", 65, [128, 16, E], BF16)
        k_blk = [sb("k_blk0", 65, [128, 16, 512], BF16),
                 sb("k_blk1", 81, [128, 16, 512], BF16)]
        v_blk = [sb("v_blk0", 97, [128, 4, E], BF16),
                 sb("v_blk1", 113, [128, 4, E], BF16)]
        h1T = sb("h1T", 65, [128, 16, 1024], BF16)
        x2part = sb("x2part", 65, [128, 8, 1024], F32)
        silu_sb = [sb("silu0", 97, [128, 1024], F32),
                   sb("silu1", 101, [128, 1024], F32)]
        u1_sb = [sb("u10", 105, [128, 1024], F32),
                 sb("u11", 109, [128, 1024], F32)]
        x2ld2 = [sb("x2ld0", 97, [128, E], F32),
                 sb("x2ld1", 105, [128, E], F32)]
        hdn_hi = sb("hdn_hi", 113, [128, 12, 1024], BF16)
        kstage2 = [sb("kstage0", 137, [128, 16, 512], BF16),
                   sb("kstage1", 153, [128, 16, 512], BF16),
                   sb("kstage2", 169, [128, 16, 512], BF16),
                   sb("kstage3", 185, [128, 16, 512], BF16)]
        vstage2 = [sb("vstage0", 137, [128, 4, E], BF16),
                   sb("vstage1", 153, [128, 4, E], BF16),
                   sb("vstage2", 169, [128, 4, E], BF16),
                   sb("vstage3", 185, [128, 4, E], BF16)]
        xqT_bf = sb("xqT_bf", 137, [128, 16, 1024], BF16)
        mask_sb = [sb("mask0", 137, [128, 2, 512], F32),
                   sb("mask1", 141, [128, 2, 512], F32)]
        xq_t = sb("xq_t", 145, [128, E], F32)
        h1st = sb("h1st", 153, [128, E], F32)
        p_sb = [sb("p_sb0", 161, [128, 512], BF16),
                sb("p_sb1", 162, [128, 512], BF16)]
        pt_sb = [sb("pt_sb0", 163, [128, 4, 128], BF16),
                 sb("pt_sb1", 164, [128, 4, 128], BF16)]
        gu_blk = [sb("gu0", 137, [128, 2, 16, 128], BF16),
                  sb("gu1", 145, [128, 2, 16, 128], BF16)]
        ds_b = sb("ds_b", 153, [128, E], F32)
        outst = sb("outst", 161, [128, E], F32)
        w_blk = [sb("w_blk0", 113, [128, 16, 128], BF16),
                 sb("w_blk1", 117, [128, 16, 128], BF16),
                 sb("w_blk2", 121, [128, 16, 128], BF16),
                 sb("w_blk3", 125, [128, 16, 128], BF16)]
        q_sb = sb("q_sb", 169, [128, 16, 1024], BF16)
        down_bf = [sb("down0", 169, [128, 5, 1024], BF16),
                   sb("down1", 179, [128, 5, 1024], BF16)]
        h1sl2 = [sb("h1sl0", 189, [128, 1024], F32),
                 sb("h1sl1", 201, [128, 1024], F32)]
        x2buf = [sb("x2buf0", 193, [128, 1024], F32),
                 sb("x2buf1", 197, [128, 1024], F32)]
        bnbuf = sb("bnbuf", 205, [128, 4, 24], F32)

        ps_a = es.enter_context(nc.psum_tensor("ps_a", [128, 2, 512], F32))
        ps_t = [es.enter_context(nc.psum_tensor("ps_t0", [128, 1024], BF16)),
                es.enter_context(nc.psum_tensor("ps_t1", [128, 1024], BF16))]
        ps_c = es.enter_context(nc.psum_tensor("ps_c", [128, 4, 512], F32))

        def sem(name):
            return Ctr(es.enter_context(nc.semaphore(name)))

        s_pe = sem("s_pe")
        s_act = sem("s_act")
        s_dve = sem("s_dve")
        s_skv4 = [sem(f"s_skv{m}") for m in range(4)]
        s_sh1 = sem("s_sh1")
        s_sx2b = [sem("s_sx2a"), sem("s_sx2b")]
        s_sout = sem("s_sout")
        s_junk = sem("s_junk")
        s_bar = sem("s_bar")
        s_ldc = sem("s_ldc")
        s_wk16 = [sem(f"s_wk{i}") for i in range(16)]
        s_xkv2 = [sem(f"s_xkv{m}") for m in range(4)]
        s_wv16 = [sem(f"s_wv{i}") for i in range(16)]
        s_xqt16 = [sem(f"s_xqt{i}") for i in range(16)]
        s_xqt = sem("s_xqt")
        s_wq2 = [sem(f"s_wqq{i}") for i in range(4)]
        s_h1t2 = [sem("s_h1t0"), sem("s_h1t1")]
        s_gu2 = [sem("s_gu0"), sem("s_gu1")]
        s_ds = sem("s_ds")
        s_dn2 = [sem("s_dn0"), sem("s_dn1")]
        s_k2 = [sem("s_k0"), sem("s_k1")]
        s_v2 = [sem("s_v0"), sem("s_v1")]
        s_mk2 = [sem("s_mk0"), sem("s_mk1")]
        s_xq = sem("s_xq")
        s_h1s2 = [sem("s_h1sa"), sem("s_h1sb")]
        s_x2l2 = [sem("s_x2la"), sem("s_x2lb")]
        gk = [sem(f"gk{m}") for m in range(4)]
        gv = [sem(f"gv{m}") for m in range(4)]

        P = Prog()

        def _emit_final_and_block():
            def sp_final(sp, t_out=s_sout.n, t_h1=s_sh1.n,
                         t_x2=(s_sx2b[0].n, s_sx2b[1].n),
                         skv=tuple(x.n for x in s_skv4)):
                sp.wait_ge(s_sout.h, t_out)
                sp.wait_ge(s_sh1.h, t_h1)
                sp.wait_ge(s_sx2b[0].h, t_x2[0])
                sp.wait_ge(s_sx2b[1].h, t_x2[1])
                for m4 in range(4):
                    sp.wait_ge(s_skv4[m4].h, skv[m4])
            P.on("sp", sp_final)

            block = es.enter_context(nc.Block())

            @block.tensor
            def _(t):
                for fn in P.ops["pe"]:
                    fn(t)

            @block.scalar
            def _(s):
                for fn in P.ops["act"]:
                    fn(s)

            @block.vector
            def _(v):
                for fn in P.ops["dve"]:
                    fn(v)

            @block.gpsimd
            def _(g):
                for fn in P.ops["pool"]:
                    fn(g)

            @block.sync
            def _(sp):
                for fn in P.ops["sp"]:
                    fn(sp)

        def barrier():
            s_bar.n += 5
            tgt = s_bar.n

            def fn_sp(eng, tgt=tgt, so=s_sout.n, sh=s_sh1.n,
                      sx=(s_sx2b[0].n, s_sx2b[1].n),
                      skv=tuple(x.n for x in s_skv4)):
                eng.wait_ge(s_sout.h, so)
                eng.wait_ge(s_sh1.h, sh)
                eng.wait_ge(s_sx2b[0].h, sx[0])
                eng.wait_ge(s_sx2b[1].h, sx[1])
                for m4 in range(4):
                    eng.wait_ge(s_skv4[m4].h, skv[m4])
                eng.drain()
                eng.sem_inc(s_bar.h, 1)
                eng.wait_ge(s_bar.h, tgt)
            P.on("sp", fn_sp)
            for e in ("pe", "act", "dve", "pool"):
                def fn(eng, tgt=tgt):
                    eng.drain()
                    eng.sem_inc(s_bar.h, 1)
                    eng.wait_ge(s_bar.h, tgt)
                P.on(e, fn)

        def init_loads(g):
            g.dma_start(out=ident[:, :], in_=d_ident[:, :]).then_inc(s_ldc.h, 16)
            g.dma_start(out=gs_sb[:, :], in_=d_gs[:, :]).then_inc(s_ldc.h, 16)
            g.dma_start(out=us_sb[:, :], in_=d_us[:, :]).then_inc(s_ldc.h, 16)
            g.memset(stats[:, ST_EPS:ST_EPS + 1], EPS)
            g.dma_start(out=xkvT_bf[:, :, 0, :],
                        in_=d_xkvT[:, :, 0, :]).then_inc(s_xkv2[0].h, 16)
            for dc in range(16):
                g.dma_start(
                    out=wk_sb[:, dc, :, :], in_=d_wkT[dc, :, :, :]
                ).then_inc(s_wk16[dc].h, 16)
            for m in range(1, 4):
                g.dma_start(out=xkvT_bf[:, :, m, :],
                            in_=d_xkvT[:, :, m, :]).then_inc(s_xkv2[m].h, 16)
        P.on("pool", init_loads)
        s_ldc.n += 48
        for i in range(16):
            s_wk16[i].n += 16
        for m in range(4):
            s_xkv2[m].n += 16
        KV_IN_READY = s_ldc.n

        kev = {}
        kst_dma = {}
        for m in range(4 if want("kvk") else 0):
            for dc in range(16):
                bank = dc % 2
                guard = kev.get((m, dc - 2)) or kev.get((m - 1, 14 + dc))

                def pe_k(t, m=m, dc=dc, bank=bank, guard=guard):
                    if dc == 0:
                        t.wait_ge(s_xkv2[m].h, 16)
                        if m == 0:
                            t.wait_ge(s_ldc.h, KV_IN_READY)
                    if m == 0:
                        t.wait_ge(s_wk16[dc].h, 16)
                    if guard:
                        t.wait_ge(s_act.h, guard)
                    for ec in range(16):
                        mm = t.matmul(ps_a[:, bank, :], wk_sb[:, dc, ec, :],
                                      xkvT_bf[:, ec, m, :],
                                      start=(ec == 0), stop=(ec == 15))
                    mm.then_inc(s_pe.h, 1)
                P.on("pe", pe_k)
                s_pe.n += 1
                pm = s_pe.n
                def act_k(s, m=m, dc=dc, bank=bank, pm=pm):
                    s.wait_ge(s_pe.h, pm)
                    s.copy(kstage2[m][:, dc, :],
                           ps_a[:, bank, :]).then_inc(s_act.h, 1)
                P.on("act", act_k)
                s_act.n += 1
                kev[(m, dc)] = s_act.n

            am = s_act.n

            def sp_kst(sp, m=m, am=am):
                sp.wait_ge(s_act.h, am)
                sp.dma_start(
                    out=kg_in[m].ap().rearrange("(dc p) t -> p dc t", p=128),
                    in_=kstage2[m][:, :, :],
                ).then_inc(s_skv4[m].h, 16)
            P.on("sp", sp_kst)
            s_skv4[m].n += 16
            kst_dma[m] = (m, s_skv4[m].n)

            if m == 3:
                K_PE_DONE = s_pe.n

                def v_loads(g, kpd=K_PE_DONE):
                    g.wait_ge(s_pe.h, kpd)
                    for ec in range(16):
                        g.dma_start(
                            out=wv_sb[:, ec, :], in_=d_wvT[ec, :, :],
                        ).then_inc(s_wv16[ec].h, 16)
                P.on("pool", v_loads)
                for i in range(16):
                    s_wv16[i].n += 16

            def pool_gk(g, m=m, tgt=(m, s_skv4[m].n)):
                g.wait_ge(s_skv4[tgt[0]].h, tgt[1])
                g.collective_compute(
                    "AllGather", ALU.bypass,
                    replica_groups=[list(range(NCORES))],
                    ins=[kg_in[m].ap().opt()],
                    outs=[kg_out[m].ap().opt()],
                ).then_inc(gk[m].h)
            P.on("pool", pool_gk)
            gk[m].n += 1

        if not want("kv"):
            barrier()
            _emit_final_and_block()
            return nc

        vev = {}
        vst_dma = {}
        vidx = 0
        for m in range(4):
            for tc in range(4):
                for dh in range(2):
                    bp = vidx % 2
                    guard = vev.get(vidx - 2)
                    vidx += 1

                    def pe_v(t, m=m, tc=tc, dh=dh, bp=bp, guard=guard,
                             first=(m == 0 and tc == 0 and dh == 0)):
                        if guard:
                            t.wait_ge(s_act.h, guard)
                        n = 0
                        for ec in range(16):
                            if first:
                                t.wait_ge(s_wv16[ec].h, 16)
                            for dq in range(2):
                                n += 1
                                mm = t.matmul(
                                    ps_c[:, 2 * bp + dq, :],
                                    xkvT_bf[:, ec, m, tc * 128:(tc + 1) * 128],
                                    wv_sb[:, ec, dh * 1024 + dq * 512:
                                          dh * 1024 + (dq + 1) * 512],
                                    start=(ec == 0), stop=(ec == 15),
                                    skip_group_check=True)
                                if n == 16:
                                    mm.then_inc(s_junk.h, 1)
                        mm.then_inc(s_pe.h, 1)
                    P.on("pe", pe_v)
                    s_junk.n += 1
                    s_pe.n += 1
                    pm = s_pe.n
                    stg = kst_dma.get(m) if (tc == 0 and dh == 0) else None

                    def act_v(s, m=m, tc=tc, dh=dh, bp=bp, pm=pm, stg=stg):
                        s.wait_ge(s_pe.h, pm)
                        if stg:
                            s.wait_ge(s_skv4[stg[0]].h, stg[1])
                        s.copy(
                            vstage2[m][:, tc, dh * 1024:(dh + 1) * 1024],
                            ps_c[:, 2 * bp:2 * bp + 2, :],
                        ).then_inc(s_act.h, 1)
                    P.on("act", act_v)
                    s_act.n += 1
                    vev[vidx - 1] = s_act.n

            am = s_act.n

            def sp_vst(sp, m=m, am=am):
                sp.wait_ge(s_act.h, am)
                sp.dma_start(
                    out=vg_in[m].ap().rearrange("(tc p) d -> p tc d", p=128),
                    in_=vstage2[m][:, :, :],
                ).then_inc(s_skv4[m].h, 16)
            P.on("sp", sp_vst)
            s_skv4[m].n += 16
            vst_dma[m] = (m, s_skv4[m].n)

            if m == 3 and want("qproj0"):
                def pool_xqt0(g):
                    g.wait_ge(s_skv4[0].h, 32)
                    g.wait_ge(s_skv4[1].h, 32)
                    for ec in range(16):
                        g.dma_start(out=xqT_bf[:, ec, :],
                                    in_=d_xqT[0, :, ec, :]).then_inc(
                                        s_xqt16[ec].h, 16)
                P.on("pool", pool_xqt0)
                for i in range(16):
                    s_xqt16[i].n += 16

            def pool_gv(g, m=m, tgt=(m, s_skv4[m].n)):
                g.wait_ge(s_skv4[tgt[0]].h, tgt[1])
                g.collective_compute(
                    "AllGather", ALU.bypass,
                    replica_groups=[list(range(NCORES))],
                    ins=[vg_in[m].ap().opt()],
                    outs=[vg_out[m].ap().opt()],
                ).then_inc(gv[m].h)
            P.on("pool", pool_gv)
            gv[m].n += 1

        if debug:
            def dbg_g(sp):
                sp.wait_ge(gk[0].h, 1)
                sp.dma_start(out=dbg["kg0"].ap(), in_=kg_out[0].ap()).then_inc(s_sout.h, 16)
                sp.wait_ge(gv[0].h, 1)
                sp.dma_start(out=dbg["vg0"].ap(), in_=vg_out[0].ap()).then_inc(s_sout.h, 16)
            P.on("sp", dbg_g)
            s_sout.n += 32

        barrier()

        def emit_pass(p):
            NKB = 16 * (p + 1)
            ret = [RET1[j] + 16 * p for j in range(8)]

            def q_loads(g, p=p):
                for ec in range(16):
                    if p == 0:
                        break
                    g.dma_start(out=xqT_bf[:, ec, :],
                                in_=d_xqT[p, :, ec, :]).then_inc(s_xqt16[ec].h, 16)
                g.memset(stats[:, ST_M:ST_M + 8], -1.0e30)
                g.memset(stats[:, ST_L:ST_L + 8], 0.0)
                g.memset(o_acc[:, :, :], 0.0)
            P.on("pool", q_loads)
            if p == 1:
                for i in range(16):
                    s_xqt16[i].n += 16
            XQT_TGT = 16 * (p + 1)

            wq_pe = {}
            qev = {}
            wq_ld = {}
            for dc in range(16):
                guard = wq_pe.get(dc - 4)

                def pool_wq(g, dc=dc, guard=guard):
                    if guard:
                        g.wait_ge(s_pe.h, guard)
                    g.dma_start(
                        out=w_blk[dc % 4][:, :, :], in_=d_wqT[dc, :, :, :]
                    ).then_inc(s_wq2[dc % 4].h, 16)
                P.on("pool", pool_wq)
                s_wq2[dc % 4].n += 16
                wq_ld[dc] = (dc % 4, s_wq2[dc % 4].n)

                evguard = qev.get(dc - 2)

                def pe_q(t, dc=dc, ldm=wq_ld[dc], evg=evguard, xb=XQT_TGT):
                    t.wait_ge(s_wq2[ldm[0]].h, ldm[1])
                    if evg:
                        t.wait_ge(s_act.h, evg)
                    bp = dc % 2
                    n = 0
                    for ec in range(16):
                        if dc == 0:
                            t.wait_ge(s_xqt16[ec].h, xb)
                        for qh in range(2):
                            n += 1
                            mm = t.matmul(
                                ps_c[:, 2 * bp + qh, :],
                                w_blk[dc % 4][:, ec, :],
                                xqT_bf[:, ec, qh * 512:(qh + 1) * 512],
                                start=(ec == 0), stop=(ec == 15),
                                skip_group_check=True)
                            if n == 16:
                                mm.then_inc(s_junk.h, 1)
                    mm.then_inc(s_pe.h, 1)
                P.on("pe", pe_q)
                s_junk.n += 1
                s_pe.n += 1
                wq_pe[dc] = s_pe.n

                def act_q(s, dc=dc, pm=s_pe.n):
                    s.wait_ge(s_pe.h, pm)
                    s.activation(
                        q_sb[:, dc, :],
                        ps_c[:, 2 * (dc % 2):2 * (dc % 2) + 2, :].rearrange(
                            "pp a b -> pp (a b)"),
                        AF.Copy, scale=SCALE,
                    ).then_inc(s_act.h, 1)
                P.on("act", act_q)
                s_act.n += 1
                qev[dc] = s_act.n

            if debug:
                def dbg_q(sp, p=p, am=s_act.n):
                    sp.wait_ge(s_act.h, am)
                    sp.dma_start(out=dbg["q"][p, :, :, :], in_=q_sb[:, :, :]).then_inc(s_sout.h, 16)
                P.on("sp", dbg_q)
                s_sout.n += 16

            kld, vld = {}, {}
            kld_guard, vld_guard = {}, {}

            def emit_kv_load(kb, p=p):
                m, r = kb // 8, kb % 8
                pe_now = s_pe.n

                def sp_k(sp, kb=kb, m=m, r=r, pe_now=pe_now):
                    sp.wait_ge(gk[m].h, 1)
                    if kb >= 2:
                        sp.wait_ge(s_pe.h, kld_guard[kb - 2])
                    else:
                        sp.wait_ge(s_pe.h, pe_now)
                    sp.dma_start(
                        out=k_blk[kb % 2][:, :, :],
                        in_=kg_out[m][r * E:(r + 1) * E, :].rearrange(
                            "(dc pp) t -> pp dc t", pp=128),
                    ).then_inc(s_k2[kb % 2].h, 16)
                P.on("sp", sp_k)
                s_k2[kb % 2].n += 16
                kld[kb] = (kb % 2, s_k2[kb % 2].n)

                def sp_v(sp, kb=kb, m=m, r=r, pe_now=pe_now):
                    sp.wait_ge(gv[m].h, 1)
                    if kb >= 2:
                        sp.wait_ge(s_pe.h, vld_guard[kb - 2])
                    else:
                        sp.wait_ge(s_pe.h, pe_now)
                    sp.dma_start(
                        out=v_blk[kb % 2][:, :, :],
                        in_=vg_out[m][r * 512:(r + 1) * 512, :].rearrange(
                            "(tc pp) d -> pp tc d", pp=128),
                    ).then_inc(s_v2[kb % 2].h, 16)
                P.on("sp", sp_v)
                s_v2[kb % 2].n += 16
                vld[kb] = (kb % 2, s_v2[kb % 2].n)

            if p == 0 and want("flash0"):
                emit_kv_load(0)
                emit_kv_load(1)

            barrier()

            if not want(f"flash{p}"):
                return False

            updates = []
            for kb in range(NKB):
                for j in range(8):
                    if ret[j] >= kb:
                        w = 0 if kb == ret[j] - 1 else (1 if kb == ret[j] else None)
                        updates.append((kb, j, w))
            if flash_trunc:
                updates = updates[:flash_trunc]
                updates = [(kb, j, None) for (kb, j, w) in updates]
                ret = [99] * 8
            NU = len(updates)

            sc_mark = [0] * NU
            tr_mark = [0] * NU
            av_mark = [0] * NU
            exp_mark = [0] * NU
            ptv_mark = [0] * NU
            nm_mark = [0] * NU
            oup_mark = [0] * NU
            msk_mark = [0] * NU
            mask_loaded = {}
            mask_guard = {}
            xq_loaded = {}
            xq_guard = [0]
            oup_last = {}
            rstd_ready = {}
            h1_ready = {}
            h1_written = [0]
            epi_queue = []

            def emit_mask_load(j, p=p):
                g2 = mask_guard.get(j - 2, 0)

                def sp_m(sp, j=j, p=p, g2=g2):
                    if g2:
                        sp.wait_ge(s_dve.h, g2)
                    sp.dma_start(
                        out=mask_sb[j % 2][:, :, :],
                        in_=d_masks[p, j, :, :, :].rearrange("k pp t -> pp k t"),
                    ).then_inc(s_mk2[j % 2].h, 16)
                P.on("sp", sp_m)
                s_mk2[j % 2].n += 16
                mask_loaded[j] = (j % 2, s_mk2[j % 2].n)

            def emit_xq_load(j, p=p):
                g2 = xq_guard[-1]

                def sp_x(sp, j=j, p=p, g2=g2):
                    if g2:
                        sp.wait_ge(s_dve.h, g2)
                    sp.dma_start(
                        out=xq_t[:, :],
                        in_=d_xq[(p * 8 + j) * 128:(p * 8 + j + 1) * 128, :],
                    ).then_inc(s_xq.h, 16)
                P.on("sp", sp_x)
                s_xq.n += 16
                xq_loaded[j] = s_xq.n

            def emit_scores(i):
                kb, j, w = updates[i]
                if kb not in kld:
                    emit_kv_load(kb)

                def pe_sc(t, i=i, kb=kb, j=j):
                    t.wait_ge(s_k2[kld[kb][0]].h, kld[kb][1])
                    if i >= 2:
                        t.wait_ge(s_act.h, exp_mark[i - 2])
                    for dc in range(16):
                        mm = t.matmul(ps_a[:, i % 2, :],
                                      q_sb[:, dc, j * 128:(j + 1) * 128],
                                      k_blk[kb % 2][:, dc, :],
                                      start=(dc == 0), stop=(dc == 15))
                    mm.then_inc(s_pe.h, 1)
                P.on("pe", pe_sc)
                s_pe.n += 1
                sc_mark[i] = s_pe.n
                kld_guard[kb] = s_pe.n

            def emit_tr(i):
                def pe_tr(t, i=i):
                    t.wait_ge(s_act.h, exp_mark[i])
                    if i >= 2:
                        t.wait_ge(s_act.h, ptv_mark[i - 2])
                    for kc in range(4):
                        tr = t.transpose(ps_t[i % 2][:, kc * 128:(kc + 1) * 128],
                                         p_sb[i % 2][:, kc * 128:(kc + 1) * 128],
                                         ident[:, :])
                    tr.then_inc(s_pe.h, 1)
                P.on("pe", pe_tr)
                s_pe.n += 1
                tr_mark[i] = s_pe.n

            def emit_av(i):
                kb, j, w = updates[i]

                def pe_av(t, i=i, kb=kb):
                    t.wait_ge(s_v2[vld[kb][0]].h, vld[kb][1])
                    t.wait_ge(s_act.h, ptv_mark[i])
                    if i >= 1:
                        t.wait_ge(s_dve.h, oup_mark[i - 1])
                    n = 0
                    for kc in range(4):
                        for dq in range(4):
                            n += 1
                            mm = t.matmul(ps_c[:, dq, :],
                                          pt_sb[i % 2][:, kc, :],
                                          v_blk[kb % 2][:, kc, dq * 512:(dq + 1) * 512],
                                          start=(kc == 0), stop=(kc == 3),
                                          skip_group_check=True)
                            if n == 8:
                                mm.then_inc(s_junk.h, 1)
                    mm.then_inc(s_pe.h, 1)
                P.on("pe", pe_av)
                s_junk.n += 1
                s_pe.n += 1
                av_mark[i] = s_pe.n
                vld_guard[kb] = s_pe.n

            def emit_bm(i, need_sep):
                kb, j, w = updates[i]
                if w is not None:

                    def dve_mask(v, i=i, j=j, w=w, sj=s_dve.n):
                        v.wait_ge(s_dve.h, sj)
                        v.wait_ge(s_pe.h, sc_mark[i])
                        v.wait_ge(s_mk2[mask_loaded[j][0]].h, mask_loaded[j][1])
                        v.tensor_tensor(ps_a[:, i % 2, :], ps_a[:, i % 2, :],
                                        mask_sb[j % 2][:, w, :],
                                        ALU.add).then_inc(s_dve.h, 1)
                    P.on("dve", dve_mask)
                    s_dve.n += 1
                    msk_mark[i] = s_dve.n
                    mask_guard[j] = s_dve.n

                def dve_bm(v, i=i, j=j, sj=s_dve.n):
                    v.wait_ge(s_dve.h, sj)
                    v.wait_ge(s_pe.h, sc_mark[i])
                    v.reduce_max(stats[:, ST_BM + j:ST_BM + j + 1],
                                 ps_a[:, i % 2, :], axis=AX.X).then_inc(s_dve.h, 1)
                P.on("dve", dve_bm)
                s_dve.n += 1

            def emit_lm(i):
                kb, j, w = updates[i]

                def dve_lm(v, i=i, j=j, sj=s_dve.n):
                    v.wait_ge(s_dve.h, sj)
                    v.wait_ge(s_act.h, exp_mark[i])
                    v.tensor_copy(stats[:, ST_CSV + j:ST_CSV + j + 1],
                                  stats[:, ST_CORR + j:ST_CORR + j + 1])
                    v.drain()
                    v.scalar_tensor_tensor(
                        stats[:, ST_L + j:ST_L + j + 1],
                        stats[:, ST_L + j:ST_L + j + 1],
                        stats[:, ST_CORR + j:ST_CORR + j + 1],
                        stats[:, ST_BS + j:ST_BS + j + 1],
                        op0=ALU.mult, op1=ALU.add)
                    v.drain()
                    v.tensor_scalar_mul(stats[:, ST_M + j:ST_M + j + 1],
                                        stats[:, ST_NM + j:ST_NM + j + 1],
                                        -1.0).then_inc(s_dve.h, 1)
                P.on("dve", dve_lm)
                s_dve.n += 1

            def emit_nm(i, sep):
                kb, j, w = updates[i]

                def dve_nm(v, i=i, j=j, sep=sep, sj=s_dve.n):
                    v.wait_ge(s_dve.h, sj)
                    v.tensor_scalar(
                        stats[:, ST_NM + j:ST_NM + j + 1],
                        stats[:, ST_BM + j:ST_BM + j + 1],
                        stats[:, ST_M + j:ST_M + j + 1], -1.0,
                        op0=ALU.max, op1=ALU.mult,
                    ).then_inc(s_dve.h, 1)
                P.on("dve", dve_nm)
                s_dve.n += 1
                nm_mark[i] = s_dve.n

            def emit_o(i):
                kb, j, w = updates[i]

                def dve_o(v, i=i, j=j, sj=s_dve.n):
                    v.wait_ge(s_dve.h, sj)
                    v.wait_ge(s_pe.h, av_mark[i])
                    v.scalar_tensor_tensor(
                        o_acc[:, j, :], o_acc[:, j, :],
                        stats[:, ST_CSV + j:ST_CSV + j + 1],
                        ps_c[:, :, :].rearrange("pp a b -> pp (a b)"),
                        op0=ALU.mult, op1=ALU.add,
                    ).then_inc(s_dve.h, 1)
                P.on("dve", dve_o)
                s_dve.n += 1
                oup_mark[i] = s_dve.n

            def emit_ptv(i):
                def act_ptv(s, i=i):
                    s.wait_ge(s_pe.h, tr_mark[i])
                    s.copy(
                        pt_sb[i % 2][:, :, :],
                        ps_t[i % 2][:, :512].rearrange(
                            "pp (a b) -> pp a b", a=4),
                    ).then_inc(s_act.h, 1)
                P.on("act", act_ptv)
                s_act.n += 1
                ptv_mark[i] = s_act.n

            def emit_corr_exp(k):
                kb, j, w = updates[k]

                def act_corr(s, k=k, j=j):
                    s.wait_ge(s_dve.h, nm_mark[k])
                    s.activation(stats[:, ST_CORR + j:ST_CORR + j + 1],
                                 stats[:, ST_M + j:ST_M + j + 1], AF.Exp,
                                 bias=stats[:, ST_NM + j:ST_NM + j + 1],
                                 scale=1.0).then_inc(s_act.h, 1)
                P.on("act", act_corr)
                s_act.n += 1

                def act_exp(s, k=k, j=j, w=w):
                    if w is not None:
                        s.wait_ge(s_dve.h, msk_mark[k])
                    s.activation(p_sb[k % 2][:, :], ps_a[:, k % 2, :], AF.Exp,
                                 bias=stats[:, ST_NM + j:ST_NM + j + 1], scale=1.0,
                                 accum_out=stats[:, ST_BS + j:ST_BS + j + 1],
                                 ).then_inc(s_act.h, 1)
                P.on("act", act_exp)
                s_act.n += 1
                exp_mark[k] = s_act.n

            def run_epi_step():
                if not epi_queue:
                    return
                j, step = epi_queue[0]
                epi_queue[0][1] += 1
                if step == 0:
                    def dve_e0(v, j=j, sj=s_dve.n):
                        v.wait_ge(s_dve.h, sj)
                        v.wait_ge(s_dve.h, oup_last[j])
                        v.reciprocal(stats[:, ST_RECIP:ST_RECIP + 1],
                                     stats[:, ST_L + j:ST_L + j + 1]
                                     ).then_inc(s_dve.h, 1)
                    P.on("dve", dve_e0)
                    s_dve.n += 1
                elif step == 1:
                    def dve_e1(v, j=j, sj=s_dve.n):
                        v.wait_ge(s_dve.h, sj)
                        v.wait_ge(s_xq.h, xq_loaded[j])
                        v.scalar_tensor_tensor(
                            o_acc[:, j, :], o_acc[:, j, :],
                            stats[:, ST_RECIP:ST_RECIP + 1], xq_t[:, :],
                            op0=ALU.mult, op1=ALU.add,
                        ).then_inc(s_dve.h, 1)
                    P.on("dve", dve_e1)
                    s_dve.n += 1
                    xq_guard.append(s_dve.n)
                elif step == 2:
                    def dve_e2(v, j=j, sj=s_dve.n):
                        v.wait_ge(s_dve.h, sj)
                        for k in range(4):
                            v.bn_stats(bnbuf[:, 2, k * 6:(k + 1) * 6],
                                       o_acc[:, j, k * 512:(k + 1) * 512])
                        v.drain()
                        v.sem_inc(s_dve.h, 1)
                    P.on("dve", dve_e2)
                    s_dve.n += 1
                elif step == 3:
                    def dve_e3(v, j=j, sj=s_dve.n):
                        v.wait_ge(s_dve.h, sj)
                        v.bn_aggr(stats[:, ST_MEAN:ST_MEAN + 2], bnbuf[:, 2, :])
                        v.drain()
                        v.sem_inc(s_dve.h, 1)
                    P.on("dve", dve_e3)
                    s_dve.n += 1
                    vm = s_dve.n

                    def act_e3(s, j=j, vm=vm):
                        s.wait_ge(s_dve.h, vm)
                        s.activation(stats[:, ST_LNV:ST_LNV + 1],
                                     stats[:, ST_VAR:ST_VAR + 1], AF.Ln,
                                     bias=stats[:, ST_EPS:ST_EPS + 1], scale=1.0)
                        s.drain()
                        s.activation(stats[:, ST_RSTD:ST_RSTD + 1],
                                     stats[:, ST_LNV:ST_LNV + 1], AF.Exp,
                                     bias=0.0, scale=-0.5).then_inc(s_act.h, 1)
                    P.on("act", act_e3)
                    s_act.n += 1
                    rstd_ready[j] = s_act.n
                elif step == 4:
                    hw = h1_written[-1]

                    def dve_e4(v, j=j, hw=hw, sj=s_dve.n):
                        v.wait_ge(s_dve.h, sj)
                        if hw:
                            v.wait_ge(s_sh1.h, hw)
                        v.wait_ge(s_act.h, rstd_ready[j])
                        v.tensor_scalar(
                            h1st[:, :], o_acc[:, j, :],
                            stats[:, ST_MEAN:ST_MEAN + 1],
                            stats[:, ST_RSTD:ST_RSTD + 1],
                            op0=ALU.subtract, op1=ALU.mult,
                        ).then_inc(s_dve.h, 1)
                    P.on("dve", dve_e4)
                    s_dve.n += 1
                    h1_ready[j] = s_dve.n
                elif step == 5:
                    def sp_h1(g, j=j):
                        g.wait_ge(s_dve.h, h1_ready[j])
                        g.dma_start(
                            out=h1_dram[j * 128:(j + 1) * 128, :], in_=h1st[:, :],
                        ).then_inc(s_sh1.h, 16)
                    P.on("pool", sp_h1)
                    s_sh1.n += 16
                    h1_written.append(s_sh1.n)
                    epi_queue.pop(0)

            for i0 in (0, 1):
                kb0, j0, w0 = updates[i0]
                if w0 == 0 and j0 not in mask_loaded:
                    emit_mask_load(j0)
            emit_scores(0)
            if NU > 1:
                emit_scores(1)
            emit_bm(0, need_sep=True)
            emit_nm(0, sep=True)
            emit_corr_exp(0)
            nm_postponed = set()

            for i in range(NU):
                kb, j, w = updates[i]
                if i + 1 < NU:
                    emit_bm(i + 1, need_sep=False)
                if i >= 1:
                    emit_lm(i - 1)
                if i in nm_postponed:
                    emit_nm(i, sep=True)
                if i + 1 < NU:
                    if updates[i + 1][1] == updates[i][1]:
                        nm_postponed.add(i + 1)
                    else:
                        emit_nm(i + 1, sep=True)
                if i >= 1:
                    emit_o(i - 1)
                if i >= 1:
                    emit_ptv(i - 1)
                if i + 1 < NU:
                    emit_corr_exp(i + 1)
                run_epi_step()
                emit_tr(i)
                if i >= 1:
                    emit_av(i - 1)
                if i + 2 < NU:
                    kb2, j2, w2 = updates[i + 2]
                    if w2 == 0 and j2 not in mask_loaded:
                        emit_mask_load(j2)
                    emit_scores(i + 2)
                if kb == ret[j]:
                    emit_xq_load(j)
                    epi_queue.append([j, 0])

            iN = NU - 1
            emit_ptv(iN)
            emit_av(iN)
            emit_lm(iN)
            emit_o(iN)

            for i2 in range(NU):
                kb2, j2, _ = updates[i2]
                if kb2 == ret[j2]:
                    oup_last[j2] = oup_mark[i2]

            while epi_queue:
                run_epi_step()

            if debug:
                def dbg_oa(sp, p=p, dm=s_dve.n):
                    sp.wait_ge(s_dve.h, dm)
                    sp.dma_start(out=dbg["oacc"][p, :, :, :], in_=o_acc[:, :, :]).then_inc(s_sout.h, 16)
                P.on("sp", dbg_oa)
                s_sout.n += 16

                def dbg_h1(sp, p=p, hm=s_sh1.n):
                    sp.wait_ge(s_sh1.h, hm)
                    sp.dma_start(out=dbg["h1"][p, :, :], in_=h1_dram.ap()).then_inc(s_sout.h, 16)
                P.on("sp", dbg_h1)
                s_sout.n += 16

            barrier()

            if not want(f"ffn{p}"):
                return False

            H1_ALL = s_sh1.n

            h1bf_guard = {}
            trv = []
            for tt in range(8):
                g2 = h1bf_guard.get(tt - 2, 0)

                def pool_h1(g, tt=tt, g2=g2, h1a=H1_ALL):
                    g.wait_ge(s_sh1.h, h1a)
                    if g2:
                        g.wait_ge(s_pe.h, g2)
                    g.dma_start(
                        out=gu_blk[tt % 2][:, :, :, :].rearrange(
                            "pp a b c -> pp (a b c)")[:, 0:2048],
                        in_=h1_dram[tt * 128:(tt + 1) * 128, :],
                    ).then_inc(s_h1t2[tt % 2].h, 16)
                P.on("pool", pool_h1)
                s_h1t2[tt % 2].n += 16
                ldm = (tt % 2, s_h1t2[tt % 2].n)
                base = s_pe.n

                def pe_h1t(t, tt=tt, ldm=ldm, base=base):
                    t.wait_ge(s_h1t2[ldm[0]].h, ldm[1])
                    src = gu_blk[tt % 2][:, :, :, :].rearrange(
                        "pp a b c -> pp (a b c)")[:, 0:2048]
                    for hc in range(16):
                        gidx = tt * 16 + hc
                        if gidx >= 2:
                            t.wait_ge(s_act.h, trv[gidx - 2])
                        tr = t.transpose(ps_t[hc % 2][:, 0:128],
                                         src[:, hc * 128:(hc + 1) * 128],
                                         ident[:, :])
                        tr.then_inc(s_pe.h, 1)
                P.on("pe", pe_h1t)
                s_pe.n += 16
                h1bf_guard[tt] = s_pe.n

                for hc in range(16):
                    def act_h1t(s, tt=tt, hc=hc, pm=base + hc + 1):
                        s.wait_ge(s_pe.h, pm)
                        s.copy(h1T[:, hc, tt * 128:(tt + 1) * 128],
                               ps_t[hc % 2][:, 0:128]).then_inc(s_act.h, 1)
                    P.on("act", act_h1t)
                    s_act.n += 1
                    trv.append(s_act.n)
            H1T_ACT = s_act.n

            gu_ld = {}
            gu_pe = {}
            silu_m = {}
            u1_m = {}
            hdn_m = {}
            for j in range(44):
                g2 = gu_pe.get(j - 2, 0) if j >= 2 else h1bf_guard[6 + j]

                def pool_gu(g, j=j, g2=g2):
                    if g2:
                        g.wait_ge(s_pe.h, g2)
                    g.dma_start(out=gu_blk[j % 2][:, 0, :, :],
                                in_=d_gateT[j, :, :, :]).then_inc(s_gu2[j % 2].h, 16)
                    g.dma_start(out=gu_blk[j % 2][:, 1, :, :],
                                in_=d_upT[j, :, :, :]).then_inc(s_gu2[j % 2].h, 16)
                P.on("pool", pool_gu)
                s_gu2[j % 2].n += 32
                gu_ld[j] = (j % 2, s_gu2[j % 2].n)

                sg = silu_m.get(j - 1, 0)
                ug = u1_m.get(j - 1, 0)

                def pe_gu(t, j=j, ldm=gu_ld[j], sg=sg, ug=ug, h1a=H1T_ACT):
                    t.wait_ge(s_gu2[ldm[0]].h, ldm[1])
                    if j == 0:
                        t.wait_ge(s_act.h, h1a)
                    if sg:
                        t.wait_ge(s_act.h, sg)
                    if ug:
                        t.wait_ge(s_dve.h, ug)
                    n = 0
                    for which, ps in ((0, ps_a), (1, ps_c)):
                        for hc in range(16):
                            for half in range(2):
                                n += 1
                                mm = t.matmul(
                                    ps[:, half, :],
                                    gu_blk[j % 2][:, which, hc, :],
                                    h1T[:, hc, half * 512:(half + 1) * 512],
                                    start=(hc == 0), stop=(hc == 15),
                                    skip_group_check=True)
                                if n % 16 == 0 and n != 64:
                                    mm.then_inc(s_junk.h, 1)
                    mm.then_inc(s_pe.h, 1)
                P.on("pe", pe_gu)
                s_junk.n += 3
                s_pe.n += 1
                gu_pe[j] = s_pe.n
                pm = s_pe.n

                hg2 = hdn_m.get(j - 2, 0)

                def act_silu(s, j=j, pm=pm, hg2=hg2):
                    s.wait_ge(s_pe.h, pm)
                    if hg2:
                        s.wait_ge(s_dve.h, hg2)
                    s.activation(silu_sb[j % 2][:, :],
                                 ps_a[:, 0:2, :].rearrange("pp a b -> pp (a b)"),
                                 AF.Silu, scale=gs_sb[:, j:j + 1],
                                 ).then_inc(s_act.h, 1)
                P.on("act", act_silu)
                s_act.n += 1
                silu_m[j] = s_act.n

                def dve_u1(v, j=j, pm=pm, sj=s_dve.n):
                    v.wait_ge(s_dve.h, sj)
                    v.wait_ge(s_pe.h, pm)
                    v.tensor_scalar_mul(
                        u1_sb[j % 2][:, :],
                        ps_c[:, 0:2, :].rearrange("pp a b -> pp (a b)"),
                        us_sb[:, j:j + 1]).then_inc(s_dve.h, 1)
                P.on("dve", dve_u1)
                s_dve.n += 1
                u1_m[j] = s_dve.n

                if j >= 1:
                    jd = j - 1

                    def dve_hdn(v, jd=jd, sj=s_dve.n):
                        v.wait_ge(s_dve.h, sj)
                        v.wait_ge(s_act.h, silu_m[jd])
                        tgt = (hdn_lo[:, jd, :] if jd < 32
                               else hdn_hi[:, jd - 32, :])
                        v.tensor_tensor(tgt, silu_sb[jd % 2][:, :],
                                        u1_sb[jd % 2][:, :],
                                        ALU.mult).then_inc(s_dve.h, 1)
                    P.on("dve", dve_hdn)
                    s_dve.n += 1
                    hdn_m[jd] = s_dve.n

            def dve_hdn_last(v, sj=s_dve.n):
                v.wait_ge(s_dve.h, sj)
                v.wait_ge(s_act.h, silu_m[43])
                v.tensor_tensor(hdn_hi[:, 11, :], silu_sb[43 % 2][:, :],
                                u1_sb[43 % 2][:, :], ALU.mult).then_inc(s_dve.h, 1)
            P.on("dve", dve_hdn_last)
            s_dve.n += 1
            hdn_m[43] = s_dve.n
            HDN_DONE = s_dve.n

            def pool_ds(g, hd=HDN_DONE):
                g.wait_ge(s_dve.h, hd)
                g.dma_start(out=ds_b[:, :], in_=d_ds[:, :]).then_inc(s_ds.h, 16)
            P.on("pool", pool_ds)
            s_ds.n += 16
            DS_READY = s_ds.n

            def hdn_sl(jj, tt):
                return (hdn_lo[:, jj, tt * 128:(tt + 1) * 128] if jj < 32
                        else hdn_hi[:, jj - 32, tt * 128:(tt + 1) * 128])

            dn_pe = {}
            dn_ld = {}
            x2p_m = {}
            x2_done_tt = {}
            x2buf_guard = [0, 0]
            h1sl_guard = [0, 0]
            pairrot = 0
            for qp in range(2):
                for gidx in range(9):
                    glen = DOWN_JG[gidx]
                    j0 = sum(DOWN_JG[:gidx])
                    prevkey = (qp, gidx - 2) if gidx >= 2 else (
                        (qp - 1, 8) if gidx == 0 else (qp - 1, 7))
                    g2 = dn_pe.get(prevkey, 0)

                    def pool_dn(g, qp=qp, gidx=gidx, j0=j0, glen=glen, g2=g2,
                                hd=HDN_DONE):
                        if g2:
                            g.wait_ge(s_pe.h, g2)
                        if qp == 0 and gidx == 0:
                            g.wait_ge(s_dve.h, hd)
                        g.dma_start(
                            out=down_bf[gidx % 2][:, 0:glen, :],
                            in_=d_downT[qp, j0:j0 + glen, :, :].rearrange(
                                "jj pp d -> pp jj d"),
                        ).then_inc(s_dn2[gidx % 2].h, 16)
                    P.on("pool", pool_dn)
                    s_dn2[gidx % 2].n += 16
                    dn_ld[(qp, gidx)] = (gidx % 2, s_dn2[gidx % 2].n)

                    for tt in range(8):
                        bp = pairrot % 3
                        pairrot += 1
                        evg = x2p_m.get((qp, gidx, tt - 3), 0) if tt >= 3 else (
                            x2p_m.get((qp, gidx - 1, 5 + tt), 0) if gidx >= 1 else
                            x2p_m.get((qp - 1, 8, 5 + tt), 0))

                        def pe_dn(t, qp=qp, gidx=gidx, glen=glen, j0=j0, tt=tt,
                                  bp=bp, ldm=dn_ld[(qp, gidx)], evg=evg,
                                  hd=HDN_DONE):
                            t.wait_ge(s_dn2[ldm[0]].h, ldm[1])
                            t.wait_ge(s_dve.h, hd)
                            if evg:
                                t.wait_ge(s_dve.h, evg)
                            n = 0
                            for jl in range(glen):
                                for qh in range(2):
                                    n += 1
                                    ps_dn = (ps_c[:, qh, :] if bp == 0 else
                                             ps_c[:, 2 + qh, :] if bp == 1 else
                                             ps_a[:, qh, :])
                                    mm = t.matmul(
                                        ps_dn,
                                        hdn_sl(j0 + jl, tt),
                                        down_bf[gidx % 2][:, jl,
                                                          qh * 512:(qh + 1) * 512],
                                        start=(jl == 0), stop=(jl == glen - 1),
                                        skip_group_check=True)
                                    if n % 10 == 0 and n != glen * 2:
                                        mm.then_inc(s_junk.h, 1)
                            mm.then_inc(s_pe.h, 1)
                        P.on("pe", pe_dn)
                        s_junk.n += (glen * 2 - 1) // 10
                        s_pe.n += 1
                        dn_pe[(qp, gidx)] = s_pe.n
                        pm = s_pe.n

                        def dve_x2p(v, qp=qp, gidx=gidx, tt=tt, bp=bp, pm=pm,
                                    sj=s_dve.n):
                            v.wait_ge(s_dve.h, sj)
                            v.wait_ge(s_pe.h, pm)
                            src = (ps_c[:, 0:2, :] if bp == 0 else
                                   ps_c[:, 2:4, :] if bp == 1 else
                                   ps_a[:, 0:2, :]).rearrange(
                                "pp a b -> pp (a b)")
                            if gidx == 0:
                                v.tensor_copy(x2part[:, tt, :], src)
                            else:
                                v.tensor_tensor(x2part[:, tt, :],
                                                x2part[:, tt, :], src, ALU.add)
                            v.drain()
                            v.sem_inc(s_dve.h, 1)
                        P.on("dve", dve_x2p)
                        s_dve.n += 1
                        x2p_m[(qp, gidx, tt)] = s_dve.n

                for tt in range(8):
                    hg = h1sl_guard[tt % 2]

                    def sp_h1sl(sp, qp=qp, tt=tt, hg=hg, h1a=H1_ALL):
                        sp.wait_ge(s_sh1.h, h1a)
                        if hg:
                            sp.wait_ge(s_dve.h, hg)
                        sp.dma_start(
                            out=h1sl2[tt % 2][:, :],
                            in_=h1_dram[tt * 128:(tt + 1) * 128,
                                        qp * 1024:(qp + 1) * 1024],
                        ).then_inc(s_h1s2[tt % 2].h, 16)
                    P.on("sp", sp_h1sl)
                    s_h1s2[tt % 2].n += 16
                    h1m = s_h1s2[tt % 2].n
                    accg = x2p_m[(qp, 8, tt)]

                    par = tt % 2
                    xg = x2buf_guard[par]

                    def dve_x2(v, qp=qp, tt=tt, h1m=h1m, par=par,
                               accg=accg, xg=xg, dsr=DS_READY, sj=s_dve.n):
                        v.wait_ge(s_dve.h, sj)
                        v.wait_ge(s_h1s2[par].h, h1m)
                        v.wait_ge(s_dve.h, accg)
                        v.wait_ge(s_ds.h, dsr)
                        if xg:
                            v.wait_ge(s_sx2b[par].h, xg)
                        for qh in range(2):
                            q = 2 * qp + qh
                            v.tensor_tensor(
                                x2buf[par][:, qh * 512:(qh + 1) * 512],
                                x2part[:, tt, qh * 512:(qh + 1) * 512],
                                ds_b[:, q * 512:(q + 1) * 512], ALU.mult)
                        v.drain()
                        v.tensor_tensor(x2buf[par][:, :], x2buf[par][:, :],
                                        h1sl2[par][:, :], ALU.add).then_inc(s_dve.h, 1)
                    P.on("dve", dve_x2)
                    s_dve.n += 1
                    xm = s_dve.n

                    def sp_x2(sp, tt=tt, qp=qp, xm=xm, par=par):
                        sp.wait_ge(s_dve.h, xm)
                        sp.dma_start(
                            out=x2_dram[tt * 128:(tt + 1) * 128,
                                        qp * 1024:(qp + 1) * 1024],
                            in_=x2buf[par][:, :],
                        ).then_inc(s_sx2b[par].h, 16)
                    P.on("act", sp_x2)
                    s_sx2b[par].n += 16
                    x2buf_guard[par] = s_sx2b[par].n
                    h1sl_guard[tt % 2] = s_dve.n
                    if qp == 1:
                        x2_done_tt[tt] = (par, s_sx2b[par].n)

            norm_guard = [0, 0]
            out_prev = [0]
            for tt in range(8):
                par = tt % 2
                ng = norm_guard[par]

                def sp_x2ld(sp, tt=tt, par=par, xdt=x2_done_tt[tt], ng=ng):
                    sp.wait_ge(s_sx2b[xdt[0]].h, xdt[1])
                    if ng:
                        sp.wait_ge(s_dve.h, ng)
                    sp.dma_start(
                        out=x2ld2[par][:, :],
                        in_=x2_dram[tt * 128:(tt + 1) * 128, :],
                    ).then_inc(s_x2l2[par].h, 16)
                P.on("sp", sp_x2ld)
                s_x2l2[par].n += 16
                xm = s_x2l2[par].n

                def dve_ln2(v, tt=tt, par=par, xm=xm,
                            sj=s_dve.n, b=ST2_BASE[par]):
                    v.wait_ge(s_dve.h, sj)
                    v.wait_ge(s_x2l2[par].h, xm)
                    for k in range(4):
                        v.bn_stats(bnbuf[:, par, k * 6:(k + 1) * 6],
                                   x2ld2[par][:, k * 512:(k + 1) * 512])
                    v.drain()
                    v.bn_aggr(stats[:, b + ST2_MEAN:b + ST2_MEAN + 2],
                              bnbuf[:, par, :])
                    v.drain()
                    v.sem_inc(s_dve.h, 1)
                P.on("dve", dve_ln2)
                s_dve.n += 1
                vm = s_dve.n

                def act_ln2(s, vm=vm, b=ST2_BASE[par]):
                    s.wait_ge(s_dve.h, vm)
                    s.activation(stats[:, b + ST2_LNV:b + ST2_LNV + 1],
                                 stats[:, b + ST2_VAR:b + ST2_VAR + 1], AF.Ln,
                                 bias=stats[:, ST_EPS:ST_EPS + 1], scale=1.0)
                    s.drain()
                    s.activation(stats[:, b + ST2_RSTD:b + ST2_RSTD + 1],
                                 stats[:, b + ST2_LNV:b + ST2_LNV + 1], AF.Exp,
                                 bias=0.0, scale=-0.5).then_inc(s_act.h, 1)
                P.on("act", act_ln2)
                s_act.n += 1
                am = s_act.n
                og = out_prev[-1]

                def dve_norm(v, par=par, am=am, og=og, sj=s_dve.n,
                             b=ST2_BASE[par]):
                    v.wait_ge(s_dve.h, sj)
                    v.wait_ge(s_act.h, am)
                    if og:
                        v.wait_ge(s_sout.h, og)
                    v.tensor_scalar(
                        outst[:, :], x2ld2[par][:, :],
                        stats[:, b + ST2_MEAN:b + ST2_MEAN + 1],
                        stats[:, b + ST2_RSTD:b + ST2_RSTD + 1],
                        op0=ALU.subtract, op1=ALU.mult).then_inc(s_dve.h, 1)
                P.on("dve", dve_norm)
                s_dve.n += 1
                nm2 = s_dve.n
                norm_guard[par] = nm2

                def sp_out(sp, tt=tt, nm2=nm2, p=p):
                    sp.wait_ge(s_dve.h, nm2)
                    row0 = (p * 8 + tt) * 128
                    sp.dma_start(
                        out=d_out[row0:row0 + 128, :], in_=outst[:, :],
                    ).then_inc(s_sout.h, 16)
                P.on("act", sp_out)
                s_sout.n += 16
                out_prev.append(s_sout.n)

            barrier()
            return True

        for p in range(2):
            if not want(f"qproj{p}"):
                break
            if not emit_pass(p):
                break

        _emit_final_and_block()
    return nc


def kernel(**inputs):
    in_maps, row_maps = _stage(inputs)
    nc = build_nc(debug=False)
    res = run_bass_kernel_spmd(nc, in_maps, core_ids=list(range(NCORES)))
    out = np.empty((S, E), np.float32)
    for c in range(NCORES):
        out[row_maps[c]] = res.results[c]["out"]
    return out.reshape(1, S, E)


# version 12
# speedup vs baseline: 1.0817x; 1.0237x over previous
import numpy as np
import ml_dtypes
import concourse.bass as bass
import concourse.mybir as mybir
from concourse.bass_utils import run_bass_kernel_spmd

F32 = mybir.dt.float32
BF16 = mybir.dt.bfloat16
AF = mybir.ActivationFunctionType
ALU = mybir.AluOpType
AX = mybir.AxisListType

E = 2048
S = 16384
NCORES = 8
SCALE = 1.0 / float(np.sqrt(2048.0))
EPS = 1e-5
NEG = -1.0e9
RET1 = [1, 3, 5, 7, 9, 11, 13, 15]
DOWN_JG = [5, 5, 5, 5, 5, 5, 5, 5, 4]

ST_M = 0
ST_L = 8
ST_BM = 16
ST_NM = 24
ST_CORR = 32
ST_BS = 40
ST_RECIP = 48
ST_MEAN = 49
ST_VAR = 50
ST_LNV = 51
ST_RSTD = 52
ST2_MEAN = 0
ST2_VAR = 1
ST2_LNV = 2
ST2_RSTD = 3
ST2_BASE = (56, 73)
ST_CSV = 64
ST_EPS = 72


def _pat(c):
    return [c, 15 - c, 16 + c, 31 - c, 32 + c, 47 - c, 48 + c, 63 - c]


def _stage(inputs):
    x = np.asarray(inputs["L0_attn_x"], np.float32).reshape(S, E)
    wq = np.asarray(inputs["L0_attn_Wq"], np.float32)
    wk = np.asarray(inputs["L0_attn_Wk"], np.float32)
    wv = np.asarray(inputs["L0_attn_Wv"], np.float32)
    gw = np.asarray(inputs["L0_ffn_gate_w"], np.float32)
    uw = np.asarray(inputs["L0_ffn_up_w"], np.float32)
    dw = np.asarray(inputs["L0_ffn_down_w"], np.float32)
    gs = np.asarray(inputs["L0_ffn_gate_s"], np.float32)
    us = np.asarray(inputs["L0_ffn_up_s"], np.float32)
    ds = np.asarray(inputs["L0_ffn_down_s"], np.float32)

    C = np.ascontiguousarray
    shared = {
        "wqT": C(wq.reshape(16, 128, 16, 128).transpose(0, 3, 2, 1)),
        "wkT": C(wk.reshape(16, 128, 16, 128).transpose(0, 3, 2, 1)),
        "wvT": C(wv.reshape(E, 16, 128).transpose(1, 2, 0)),
        "gateT": C(gw.reshape(44, 128, 16, 128).transpose(0, 3, 2, 1)),
        "upT": C(uw.reshape(44, 128, 16, 128).transpose(0, 3, 2, 1)),
        "downT": C(dw.reshape(2, 1024, 44, 128).transpose(0, 2, 3, 1)),
        "gate_s": C(gs.reshape(44, 128).T),
        "up_s": C(us.reshape(44, 128).T),
        "down_s": C(np.broadcast_to(ds.reshape(1, E), (128, E))),
        "ident": np.eye(128, dtype=ml_dtypes.bfloat16),
    }
    tri = np.triu(np.ones((128, 128), np.float32), k=1) * NEG

    in_maps, row_maps = [], []
    for c in range(NCORES):
        tiles = [[64 * p + t for t in _pat(c)] for p in (0, 1)]
        rows = np.concatenate([
            np.arange(t * 128, t * 128 + 128) for p in (0, 1) for t in tiles[p]
        ])
        row_maps.append(rows)
        xq = C(x[rows])
        xqT = C(
            x[rows].reshape(2, 8, 128, 16, 128).transpose(0, 4, 3, 1, 2)
        ).reshape(2, 128, 16, 1024)
        kvrows = np.concatenate([
            np.arange(4096 * m + 512 * c, 4096 * m + 512 * c + 512)
            for m in range(4)
        ])
        xkvT = C(x[kvrows].reshape(4, 512, 16, 128).transpose(3, 2, 0, 1))
        masks = np.zeros((2, 8, 2, 128, 512), np.float32)
        for p in (0, 1):
            for j in range(8):
                t = tiles[p][j]
                ret = RET1[j] + 16 * p
                for which, kb in enumerate((ret - 1, ret)):
                    for sub in range(4):
                        kt = 4 * kb + sub
                        blk = masks[p, j, which, :, sub * 128:(sub + 1) * 128]
                        if kt == t:
                            blk[:] = tri
                        elif kt > t:
                            blk[:] = NEG
        m = dict(shared)
        m.update({"xq": xq, "xqT": xqT, "xkvT": xkvT, "masks": C(masks)})
        in_maps.append(m)
    return in_maps, row_maps


class Ctr:
    def __init__(self, handle):
        self.h = handle
        self.n = 0


class Prog:
    ENGS = ("pe", "act", "dve", "pool", "sp")

    def __init__(self):
        self.ops = {e: [] for e in self.ENGS}

    def on(self, eng, fn):
        self.ops[eng].append(fn)


def build_nc(debug=False, stop_after=None, flash_trunc=0):
    _ph_order = ["init", "kvk", "kv", "qproj0", "flash0", "ffn0",
                 "qproj1", "flash1", "ffn1"]
    _stop_idx = _ph_order.index(stop_after) if stop_after else 99

    def want(ph):
        return _ph_order.index(ph) <= _stop_idx

    nc = bass.Bass("TRN2", target_bir_lowering=False, debug=False)

    d_xq = nc.dram_tensor("xq", [2048, E], F32, kind="ExternalInput")
    d_xqT = nc.dram_tensor("xqT", [2, 128, 16, 1024], F32, kind="ExternalInput")
    d_xkvT = nc.dram_tensor("xkvT", [128, 16, 4, 512], F32, kind="ExternalInput")
    d_wqT = nc.dram_tensor("wqT", [16, 128, 16, 128], F32, kind="ExternalInput")
    d_wkT = nc.dram_tensor("wkT", [16, 128, 16, 128], F32, kind="ExternalInput")
    d_wvT = nc.dram_tensor("wvT", [16, 128, E], F32, kind="ExternalInput")
    d_gateT = nc.dram_tensor("gateT", [44, 128, 16, 128], F32, kind="ExternalInput")
    d_upT = nc.dram_tensor("upT", [44, 128, 16, 128], F32, kind="ExternalInput")
    d_downT = nc.dram_tensor("downT", [2, 44, 128, 1024], F32, kind="ExternalInput")
    d_gs = nc.dram_tensor("gate_s", [128, 44], F32, kind="ExternalInput")
    d_us = nc.dram_tensor("up_s", [128, 44], F32, kind="ExternalInput")
    d_ds = nc.dram_tensor("down_s", [128, E], F32, kind="ExternalInput")
    d_ident = nc.dram_tensor("ident", [128, 128], BF16, kind="ExternalInput")
    d_masks = nc.dram_tensor("masks", [2, 8, 2, 128, 512], F32, kind="ExternalInput")
    d_out = nc.dram_tensor("out", [2048, E], F32, kind="ExternalOutput")

    kg_in = [nc.dram_tensor(f"kgin{m}", [E, 512], BF16) for m in range(4)]
    vg_in = [nc.dram_tensor(f"vgin{m}", [512, E], BF16) for m in range(4)]
    kg_out = [nc.dram_tensor(f"kgout{m}", [8 * E, 512], BF16) for m in range(4)]
    vg_out = [nc.dram_tensor(f"vgout{m}", [8 * 512, E], BF16) for m in range(4)]
    h1_dram = nc.dram_tensor("h1d", [8 * 128, E], F32)
    x2_dram = nc.dram_tensor("x2d", [8 * 128, E], F32)

    dbg = {}
    if debug:
        dbg["kg0"] = nc.dram_tensor("dbg_kg0", [8 * E, 512], BF16, kind="ExternalOutput")
        dbg["vg0"] = nc.dram_tensor("dbg_vg0", [8 * 512, E], BF16, kind="ExternalOutput")
        dbg["q"] = nc.dram_tensor("dbg_q", [2, 128, 16, 1024], BF16, kind="ExternalOutput")
        dbg["h1"] = nc.dram_tensor("dbg_h1", [2, 8 * 128, E], F32, kind="ExternalOutput")
        dbg["oacc"] = nc.dram_tensor("dbg_oacc", [2, 128, 8, E], F32, kind="ExternalOutput")

    from contextlib import ExitStack
    with ExitStack() as es:
        BASE = (nc.sbuf_base + 31) // 32 * 32
        LIMIT = nc.SBUF_PARTITION_SIZE_BYTES

        def sb(name, off_kb, shape, dt):
            off = BASE + int(off_kb * 1024)
            sz = int(np.prod(shape[1:])) * mybir.dt.size(dt)
            assert off + sz <= LIMIT, (name, off, sz, LIMIT)
            return nc.alloc_sbuf_tensor_at(name, shape, dt, offset=off)

        ident = sb("identt", 0.0, [128, 128], BF16)
        stats = sb("stats", 256 / 1024.0, [128, 80], F32)
        gs_sb = sb("gs_sb", 576 / 1024.0, [128, 44], F32)
        us_sb = sb("us_sb", 768 / 1024.0, [128, 44], F32)
        xkvT_bf = sb("xkvT_bf", 1, [128, 16, 4, 512], BF16)
        o_acc = sb("o_acc", 1, [128, 8, E], F32)
        hdn_lo = sb("hdn_lo", 1, [128, 32, 1024], BF16)
        wk_sb = sb("wk_sb", 65, [128, 16, 16, 128], BF16)
        wv_sb = sb("wv_sb", 65, [128, 16, E], BF16)
        k_blk = [sb("k_blk0", 65, [128, 16, 512], BF16),
                 sb("k_blk1", 81, [128, 16, 512], BF16)]
        v_blk = [sb("v_blk0", 97, [128, 4, E], BF16),
                 sb("v_blk1", 113, [128, 4, E], BF16)]
        h1T = sb("h1T", 65, [128, 16, 1024], BF16)
        x2part = sb("x2part", 65, [128, 8, 1024], F32)
        silu_sb = [sb("silu0", 97, [128, 1024], F32),
                   sb("silu1", 101, [128, 1024], F32)]
        u1_sb = [sb("u10", 105, [128, 1024], F32),
                 sb("u11", 109, [128, 1024], F32)]
        qp0sv = ([sb(f"qp0sv{t}", 97 + 4 * t, [128, 1024], F32)
                  for t in range(4)] +
                 [sb(f"qp0sv{t}", 137 + 4 * (t - 4), [128, 1024], F32)
                  for t in range(4, 8)])
        hdn_hi = sb("hdn_hi", 113, [128, 12, 1024], BF16)
        kstage2 = [sb("kstage0", 137, [128, 16, 512], BF16),
                   sb("kstage1", 153, [128, 16, 512], BF16),
                   sb("kstage2", 169, [128, 16, 512], BF16),
                   sb("kstage3", 185, [128, 16, 512], BF16)]
        vstage2 = [sb("vstage0", 137, [128, 4, E], BF16),
                   sb("vstage1", 153, [128, 4, E], BF16),
                   sb("vstage2", 169, [128, 4, E], BF16),
                   sb("vstage3", 185, [128, 4, E], BF16)]
        xqT_bf = sb("xqT_bf", 137, [128, 16, 1024], BF16)
        mask_sb = [sb("mask0", 137, [128, 2, 512], F32),
                   sb("mask1", 141, [128, 2, 512], F32)]
        xq_t = sb("xq_t", 145, [128, E], F32)
        h1st = sb("h1st", 153, [128, E], F32)
        p_sb = [sb("p_sb0", 161, [128, 512], BF16),
                sb("p_sb1", 162, [128, 512], BF16)]
        pt_sb = [sb("pt_sb0", 163, [128, 4, 128], BF16),
                 sb("pt_sb1", 164, [128, 4, 128], BF16)]
        gu_blk = [sb("gu0", 137, [128, 2, 16, 128], BF16),
                  sb("gu1", 145, [128, 2, 16, 128], BF16)]
        ds_b = sb("ds_b", 153, [128, E], F32)
        outst = sb("outst", 161, [128, E], F32)
        w_blk = [sb("w_blk0", 113, [128, 16, 128], BF16),
                 sb("w_blk1", 117, [128, 16, 128], BF16),
                 sb("w_blk2", 121, [128, 16, 128], BF16),
                 sb("w_blk3", 125, [128, 16, 128], BF16)]
        q_sb = sb("q_sb", 169, [128, 16, 1024], BF16)
        down_bf = [sb("down0", 169, [128, 5, 1024], BF16),
                   sb("down1", 179, [128, 5, 1024], BF16)]
        h1sl2 = [sb("h1sl0", 189, [128, 1024], F32),
                 sb("h1sl1", 201, [128, 1024], F32)]
        x2buf = [sb("x2buf0", 193, [128, 1024], F32),
                 sb("x2buf1", 197, [128, 1024], F32)]
        bnbuf = sb("bnbuf", 205, [128, 4, 24], F32)
        bnst = sb("bnst", 205.5, [128, 8, 24], F32)

        ps_a = es.enter_context(nc.psum_tensor("ps_a", [128, 2, 512], F32))
        ps_t = [es.enter_context(nc.psum_tensor("ps_t0", [128, 1024], BF16)),
                es.enter_context(nc.psum_tensor("ps_t1", [128, 1024], BF16))]
        ps_c = es.enter_context(nc.psum_tensor("ps_c", [128, 4, 512], F32))

        def sem(name):
            return Ctr(es.enter_context(nc.semaphore(name)))

        s_pe = sem("s_pe")
        s_act = sem("s_act")
        s_dve = sem("s_dve")
        s_skv4 = [sem(f"s_skv{m}") for m in range(4)]
        s_sh1 = sem("s_sh1")
        s_sx2b = [sem("s_sx2a"), sem("s_sx2b")]
        s_sout = sem("s_sout")
        s_junk = sem("s_junk")
        s_bar = sem("s_bar")
        s_ldc = sem("s_ldc")
        s_wk16 = [sem(f"s_wk{i}") for i in range(16)]
        s_xkv2 = [sem(f"s_xkv{m}") for m in range(4)]
        s_wv16 = [sem(f"s_wv{i}") for i in range(16)]
        s_xqt16 = [sem(f"s_xqt{i}") for i in range(16)]
        s_xqt = sem("s_xqt")
        s_wq2 = [sem(f"s_wqq{i}") for i in range(4)]
        s_h1t2 = [sem("s_h1t0"), sem("s_h1t1")]
        s_gu2 = [sem("s_gu0"), sem("s_gu1")]
        s_ds = sem("s_ds")
        s_dn2 = [sem("s_dn0"), sem("s_dn1")]
        s_k2 = [sem("s_k0"), sem("s_k1")]
        s_v2 = [sem("s_v0"), sem("s_v1")]
        s_mk2 = [sem("s_mk0"), sem("s_mk1")]
        s_xq = sem("s_xq")
        s_h1s2 = [sem("s_h1sa"), sem("s_h1sb")]
        s_x2l2 = [sem("s_x2la"), sem("s_x2lb")]
        gk = [sem(f"gk{m}") for m in range(4)]
        gv = [sem(f"gv{m}") for m in range(4)]

        P = Prog()

        def _emit_final_and_block():
            def sp_final(sp, t_out=s_sout.n, t_h1=s_sh1.n,
                         t_x2=(s_sx2b[0].n, s_sx2b[1].n),
                         skv=tuple(x.n for x in s_skv4)):
                sp.wait_ge(s_sout.h, t_out)
                sp.wait_ge(s_sh1.h, t_h1)
                sp.wait_ge(s_sx2b[0].h, t_x2[0])
                sp.wait_ge(s_sx2b[1].h, t_x2[1])
                for m4 in range(4):
                    sp.wait_ge(s_skv4[m4].h, skv[m4])
            P.on("sp", sp_final)

            block = es.enter_context(nc.Block())

            @block.tensor
            def _(t):
                for fn in P.ops["pe"]:
                    fn(t)

            @block.scalar
            def _(s):
                for fn in P.ops["act"]:
                    fn(s)

            @block.vector
            def _(v):
                for fn in P.ops["dve"]:
                    fn(v)

            @block.gpsimd
            def _(g):
                for fn in P.ops["pool"]:
                    fn(g)

            @block.sync
            def _(sp):
                for fn in P.ops["sp"]:
                    fn(sp)

        def barrier():
            s_bar.n += 5
            tgt = s_bar.n

            def fn_sp(eng, tgt=tgt, so=s_sout.n, sh=s_sh1.n,
                      sx=(s_sx2b[0].n, s_sx2b[1].n),
                      skv=tuple(x.n for x in s_skv4)):
                eng.wait_ge(s_sout.h, so)
                eng.wait_ge(s_sh1.h, sh)
                eng.wait_ge(s_sx2b[0].h, sx[0])
                eng.wait_ge(s_sx2b[1].h, sx[1])
                for m4 in range(4):
                    eng.wait_ge(s_skv4[m4].h, skv[m4])
                eng.drain()
                eng.sem_inc(s_bar.h, 1)
                eng.wait_ge(s_bar.h, tgt)
            P.on("sp", fn_sp)
            for e in ("pe", "act", "dve", "pool"):
                def fn(eng, tgt=tgt):
                    eng.drain()
                    eng.sem_inc(s_bar.h, 1)
                    eng.wait_ge(s_bar.h, tgt)
                P.on(e, fn)

        def init_loads(g):
            g.dma_start(out=ident[:, :], in_=d_ident[:, :]).then_inc(s_ldc.h, 16)
            g.dma_start(out=gs_sb[:, :], in_=d_gs[:, :]).then_inc(s_ldc.h, 16)
            g.dma_start(out=us_sb[:, :], in_=d_us[:, :]).then_inc(s_ldc.h, 16)
            g.memset(stats[:, ST_EPS:ST_EPS + 1], EPS)
            g.dma_start(out=xkvT_bf[:, :, 0, :],
                        in_=d_xkvT[:, :, 0, :]).then_inc(s_xkv2[0].h, 16)
            for dc in range(16):
                g.dma_start(
                    out=wk_sb[:, dc, :, :], in_=d_wkT[dc, :, :, :]
                ).then_inc(s_wk16[dc].h, 16)
            for m in range(1, 4):
                g.dma_start(out=xkvT_bf[:, :, m, :],
                            in_=d_xkvT[:, :, m, :]).then_inc(s_xkv2[m].h, 16)
        P.on("pool", init_loads)
        s_ldc.n += 48
        for i in range(16):
            s_wk16[i].n += 16
        for m in range(4):
            s_xkv2[m].n += 16
        KV_IN_READY = s_ldc.n

        kev = {}
        kst_dma = {}
        for m in range(4 if want("kvk") else 0):
            for dc in range(16):
                bank = dc % 2
                guard = kev.get((m, dc - 2)) or kev.get((m - 1, 14 + dc))

                def pe_k(t, m=m, dc=dc, bank=bank, guard=guard):
                    if dc == 0:
                        t.wait_ge(s_xkv2[m].h, 16)
                        if m == 0:
                            t.wait_ge(s_ldc.h, KV_IN_READY)
                    if m == 0:
                        t.wait_ge(s_wk16[dc].h, 16)
                    if guard:
                        t.wait_ge(s_act.h, guard)
                    for ec in range(16):
                        mm = t.matmul(ps_a[:, bank, :], wk_sb[:, dc, ec, :],
                                      xkvT_bf[:, ec, m, :],
                                      start=(ec == 0), stop=(ec == 15))
                    mm.then_inc(s_pe.h, 1)
                P.on("pe", pe_k)
                s_pe.n += 1
                pm = s_pe.n
                def act_k(s, m=m, dc=dc, bank=bank, pm=pm):
                    s.wait_ge(s_pe.h, pm)
                    s.copy(kstage2[m][:, dc, :],
                           ps_a[:, bank, :]).then_inc(s_act.h, 1)
                P.on("act", act_k)
                s_act.n += 1
                kev[(m, dc)] = s_act.n

            am = s_act.n

            def sp_kst(sp, m=m, am=am):
                sp.wait_ge(s_act.h, am)
                sp.dma_start(
                    out=kg_in[m].ap().rearrange("(dc p) t -> p dc t", p=128),
                    in_=kstage2[m][:, :, :],
                ).then_inc(s_skv4[m].h, 16)
            P.on("sp", sp_kst)
            s_skv4[m].n += 16
            kst_dma[m] = (m, s_skv4[m].n)

            if m == 3:
                K_PE_DONE = s_pe.n

                def v_loads(g, kpd=K_PE_DONE):
                    g.wait_ge(s_pe.h, kpd)
                    for ec in range(16):
                        g.dma_start(
                            out=wv_sb[:, ec, :], in_=d_wvT[ec, :, :],
                        ).then_inc(s_wv16[ec].h, 16)
                P.on("pool", v_loads)
                for i in range(16):
                    s_wv16[i].n += 16

            def pool_gk(g, m=m, tgt=(m, s_skv4[m].n)):
                g.wait_ge(s_skv4[tgt[0]].h, tgt[1])
                g.collective_compute(
                    "AllGather", ALU.bypass,
                    replica_groups=[list(range(NCORES))],
                    ins=[kg_in[m].ap().opt()],
                    outs=[kg_out[m].ap().opt()],
                ).then_inc(gk[m].h)
            P.on("pool", pool_gk)
            gk[m].n += 1

        if not want("kv"):
            barrier()
            _emit_final_and_block()
            return nc

        vev = {}
        vst_dma = {}
        vidx = 0
        for m in range(4):
            for tc in range(4):
                for dh in range(2):
                    bp = vidx % 2
                    guard = vev.get(vidx - 2)
                    vidx += 1

                    def pe_v(t, m=m, tc=tc, dh=dh, bp=bp, guard=guard,
                             first=(m == 0 and tc == 0 and dh == 0)):
                        if guard:
                            t.wait_ge(s_act.h, guard)
                        n = 0
                        for ec in range(16):
                            if first:
                                t.wait_ge(s_wv16[ec].h, 16)
                            for dq in range(2):
                                n += 1
                                mm = t.matmul(
                                    ps_c[:, 2 * bp + dq, :],
                                    xkvT_bf[:, ec, m, tc * 128:(tc + 1) * 128],
                                    wv_sb[:, ec, dh * 1024 + dq * 512:
                                          dh * 1024 + (dq + 1) * 512],
                                    start=(ec == 0), stop=(ec == 15),
                                    skip_group_check=True)
                                if n == 16:
                                    mm.then_inc(s_junk.h, 1)
                        mm.then_inc(s_pe.h, 1)
                    P.on("pe", pe_v)
                    s_junk.n += 1
                    s_pe.n += 1
                    pm = s_pe.n
                    stg = kst_dma.get(m) if (tc == 0 and dh == 0) else None

                    def act_v(s, m=m, tc=tc, dh=dh, bp=bp, pm=pm, stg=stg):
                        s.wait_ge(s_pe.h, pm)
                        if stg:
                            s.wait_ge(s_skv4[stg[0]].h, stg[1])
                        s.copy(
                            vstage2[m][:, tc, dh * 1024:(dh + 1) * 1024],
                            ps_c[:, 2 * bp:2 * bp + 2, :],
                        ).then_inc(s_act.h, 1)
                    P.on("act", act_v)
                    s_act.n += 1
                    vev[vidx - 1] = s_act.n

            am = s_act.n

            def sp_vst(sp, m=m, am=am):
                sp.wait_ge(s_act.h, am)
                sp.dma_start(
                    out=vg_in[m].ap().rearrange("(tc p) d -> p tc d", p=128),
                    in_=vstage2[m][:, :, :],
                ).then_inc(s_skv4[m].h, 16)
            P.on("sp", sp_vst)
            s_skv4[m].n += 16
            vst_dma[m] = (m, s_skv4[m].n)

            if m == 3 and want("qproj0"):
                def pool_xqt0(g):
                    g.wait_ge(s_skv4[0].h, 32)
                    g.wait_ge(s_skv4[1].h, 32)
                    for ec in range(16):
                        g.dma_start(out=xqT_bf[:, ec, :],
                                    in_=d_xqT[0, :, ec, :]).then_inc(
                                        s_xqt16[ec].h, 16)
                P.on("pool", pool_xqt0)
                for i in range(16):
                    s_xqt16[i].n += 16

            def pool_gv(g, m=m, tgt=(m, s_skv4[m].n)):
                g.wait_ge(s_skv4[tgt[0]].h, tgt[1])
                g.collective_compute(
                    "AllGather", ALU.bypass,
                    replica_groups=[list(range(NCORES))],
                    ins=[vg_in[m].ap().opt()],
                    outs=[vg_out[m].ap().opt()],
                ).then_inc(gv[m].h)
            P.on("pool", pool_gv)
            gv[m].n += 1

        if debug:
            def dbg_g(sp):
                sp.wait_ge(gk[0].h, 1)
                sp.dma_start(out=dbg["kg0"].ap(), in_=kg_out[0].ap()).then_inc(s_sout.h, 16)
                sp.wait_ge(gv[0].h, 1)
                sp.dma_start(out=dbg["vg0"].ap(), in_=vg_out[0].ap()).then_inc(s_sout.h, 16)
            P.on("sp", dbg_g)
            s_sout.n += 32

        barrier()

        def emit_pass(p):
            NKB = 16 * (p + 1)
            ret = [RET1[j] + 16 * p for j in range(8)]

            def q_loads(g, p=p):
                for ec in range(16):
                    if p == 0:
                        break
                    g.dma_start(out=xqT_bf[:, ec, :],
                                in_=d_xqT[p, :, ec, :]).then_inc(s_xqt16[ec].h, 16)
                g.memset(stats[:, ST_M:ST_M + 8], -1.0e30)
                g.memset(stats[:, ST_L:ST_L + 8], 0.0)
                g.memset(o_acc[:, :, :], 0.0)
            P.on("pool", q_loads)
            if p == 1:
                for i in range(16):
                    s_xqt16[i].n += 16
            XQT_TGT = 16 * (p + 1)

            wq_pe = {}
            qev = {}
            wq_ld = {}
            for dc in range(16):
                guard = wq_pe.get(dc - 4)

                def pool_wq(g, dc=dc, guard=guard):
                    if guard:
                        g.wait_ge(s_pe.h, guard)
                    g.dma_start(
                        out=w_blk[dc % 4][:, :, :], in_=d_wqT[dc, :, :, :]
                    ).then_inc(s_wq2[dc % 4].h, 16)
                P.on("pool", pool_wq)
                s_wq2[dc % 4].n += 16
                wq_ld[dc] = (dc % 4, s_wq2[dc % 4].n)

                evguard = qev.get(dc - 2)

                def pe_q(t, dc=dc, ldm=wq_ld[dc], evg=evguard, xb=XQT_TGT):
                    t.wait_ge(s_wq2[ldm[0]].h, ldm[1])
                    if evg:
                        t.wait_ge(s_act.h, evg)
                    bp = dc % 2
                    n = 0
                    for ec in range(16):
                        if dc == 0:
                            t.wait_ge(s_xqt16[ec].h, xb)
                        for qh in range(2):
                            n += 1
                            mm = t.matmul(
                                ps_c[:, 2 * bp + qh, :],
                                w_blk[dc % 4][:, ec, :],
                                xqT_bf[:, ec, qh * 512:(qh + 1) * 512],
                                start=(ec == 0), stop=(ec == 15),
                                skip_group_check=True)
                            if n == 16:
                                mm.then_inc(s_junk.h, 1)
                    mm.then_inc(s_pe.h, 1)
                P.on("pe", pe_q)
                s_junk.n += 1
                s_pe.n += 1
                wq_pe[dc] = s_pe.n

                def act_q(s, dc=dc, pm=s_pe.n):
                    s.wait_ge(s_pe.h, pm)
                    s.activation(
                        q_sb[:, dc, :],
                        ps_c[:, 2 * (dc % 2):2 * (dc % 2) + 2, :].rearrange(
                            "pp a b -> pp (a b)"),
                        AF.Copy, scale=SCALE,
                    ).then_inc(s_act.h, 1)
                P.on("act", act_q)
                s_act.n += 1
                qev[dc] = s_act.n

            if debug:
                def dbg_q(sp, p=p, am=s_act.n):
                    sp.wait_ge(s_act.h, am)
                    sp.dma_start(out=dbg["q"][p, :, :, :], in_=q_sb[:, :, :]).then_inc(s_sout.h, 16)
                P.on("sp", dbg_q)
                s_sout.n += 16

            kld, vld = {}, {}
            kld_guard, vld_guard = {}, {}

            def emit_kv_load(kb, p=p):
                m, r = kb // 8, kb % 8
                pe_now = s_pe.n

                def sp_k(sp, kb=kb, m=m, r=r, pe_now=pe_now):
                    sp.wait_ge(gk[m].h, 1)
                    if kb >= 2:
                        sp.wait_ge(s_pe.h, kld_guard[kb - 2])
                    else:
                        sp.wait_ge(s_pe.h, pe_now)
                    sp.dma_start(
                        out=k_blk[kb % 2][:, :, :],
                        in_=kg_out[m][r * E:(r + 1) * E, :].rearrange(
                            "(dc pp) t -> pp dc t", pp=128),
                    ).then_inc(s_k2[kb % 2].h, 16)
                P.on("sp", sp_k)
                s_k2[kb % 2].n += 16
                kld[kb] = (kb % 2, s_k2[kb % 2].n)

                def sp_v(sp, kb=kb, m=m, r=r, pe_now=pe_now):
                    sp.wait_ge(gv[m].h, 1)
                    if kb >= 2:
                        sp.wait_ge(s_pe.h, vld_guard[kb - 2])
                    else:
                        sp.wait_ge(s_pe.h, pe_now)
                    sp.dma_start(
                        out=v_blk[kb % 2][:, :, :],
                        in_=vg_out[m][r * 512:(r + 1) * 512, :].rearrange(
                            "(tc pp) d -> pp tc d", pp=128),
                    ).then_inc(s_v2[kb % 2].h, 16)
                P.on("sp", sp_v)
                s_v2[kb % 2].n += 16
                vld[kb] = (kb % 2, s_v2[kb % 2].n)

            if p == 0 and want("flash0"):
                emit_kv_load(0)
                emit_kv_load(1)

            barrier()

            if not want(f"flash{p}"):
                return False

            updates = []
            for kb in range(NKB):
                for j in range(8):
                    if ret[j] >= kb:
                        w = 0 if kb == ret[j] - 1 else (1 if kb == ret[j] else None)
                        updates.append((kb, j, w))
            if flash_trunc:
                updates = updates[:flash_trunc]
                updates = [(kb, j, None) for (kb, j, w) in updates]
                ret = [99] * 8
            NU = len(updates)

            sc_mark = [0] * NU
            tr_mark = [0] * NU
            av_mark = [0] * NU
            exp_mark = [0] * NU
            ptv_mark = [0] * NU
            nm_mark = [0] * NU
            oup_mark = [0] * NU
            msk_mark = [0] * NU
            mask_loaded = {}
            mask_guard = {}
            xq_loaded = {}
            xq_guard = [0]
            oup_last = {}
            rstd_ready = {}
            h1_ready = {}
            h1_written = [0]
            epi_queue = []

            def emit_mask_load(j, p=p):
                g2 = mask_guard.get(j - 2, 0)

                def sp_m(sp, j=j, p=p, g2=g2):
                    if g2:
                        sp.wait_ge(s_dve.h, g2)
                    sp.dma_start(
                        out=mask_sb[j % 2][:, :, :],
                        in_=d_masks[p, j, :, :, :].rearrange("k pp t -> pp k t"),
                    ).then_inc(s_mk2[j % 2].h, 16)
                P.on("sp", sp_m)
                s_mk2[j % 2].n += 16
                mask_loaded[j] = (j % 2, s_mk2[j % 2].n)

            def emit_xq_load(j, p=p):
                g2 = xq_guard[-1]

                def sp_x(sp, j=j, p=p, g2=g2):
                    if g2:
                        sp.wait_ge(s_dve.h, g2)
                    sp.dma_start(
                        out=xq_t[:, :],
                        in_=d_xq[(p * 8 + j) * 128:(p * 8 + j + 1) * 128, :],
                    ).then_inc(s_xq.h, 16)
                P.on("sp", sp_x)
                s_xq.n += 16
                xq_loaded[j] = s_xq.n

            def emit_scores(i):
                kb, j, w = updates[i]
                if kb not in kld:
                    emit_kv_load(kb)

                def pe_sc(t, i=i, kb=kb, j=j):
                    t.wait_ge(s_k2[kld[kb][0]].h, kld[kb][1])
                    if i >= 2:
                        t.wait_ge(s_act.h, exp_mark[i - 2])
                    for dc in range(16):
                        mm = t.matmul(ps_a[:, i % 2, :],
                                      q_sb[:, dc, j * 128:(j + 1) * 128],
                                      k_blk[kb % 2][:, dc, :],
                                      start=(dc == 0), stop=(dc == 15))
                    mm.then_inc(s_pe.h, 1)
                P.on("pe", pe_sc)
                s_pe.n += 1
                sc_mark[i] = s_pe.n
                kld_guard[kb] = s_pe.n

            def emit_tr(i):
                def pe_tr(t, i=i):
                    t.wait_ge(s_act.h, exp_mark[i])
                    if i >= 2:
                        t.wait_ge(s_act.h, ptv_mark[i - 2])
                    for kc in range(4):
                        tr = t.transpose(ps_t[i % 2][:, kc * 128:(kc + 1) * 128],
                                         p_sb[i % 2][:, kc * 128:(kc + 1) * 128],
                                         ident[:, :])
                    tr.then_inc(s_pe.h, 1)
                P.on("pe", pe_tr)
                s_pe.n += 1
                tr_mark[i] = s_pe.n

            def emit_av(i):
                kb, j, w = updates[i]

                def pe_av(t, i=i, kb=kb):
                    t.wait_ge(s_v2[vld[kb][0]].h, vld[kb][1])
                    t.wait_ge(s_act.h, ptv_mark[i])
                    if i >= 1:
                        t.wait_ge(s_dve.h, oup_mark[i - 1])
                    n = 0
                    for kc in range(4):
                        for dq in range(4):
                            n += 1
                            mm = t.matmul(ps_c[:, dq, :],
                                          pt_sb[i % 2][:, kc, :],
                                          v_blk[kb % 2][:, kc, dq * 512:(dq + 1) * 512],
                                          start=(kc == 0), stop=(kc == 3),
                                          skip_group_check=True)
                            if n == 8:
                                mm.then_inc(s_junk.h, 1)
                    mm.then_inc(s_pe.h, 1)
                P.on("pe", pe_av)
                s_junk.n += 1
                s_pe.n += 1
                av_mark[i] = s_pe.n
                vld_guard[kb] = s_pe.n

            def emit_bm(i, need_sep):
                kb, j, w = updates[i]
                if w is not None:

                    def dve_mask(v, i=i, j=j, w=w, sj=s_dve.n):
                        v.wait_ge(s_dve.h, sj)
                        v.wait_ge(s_pe.h, sc_mark[i])
                        v.wait_ge(s_mk2[mask_loaded[j][0]].h, mask_loaded[j][1])
                        v.tensor_tensor(ps_a[:, i % 2, :], ps_a[:, i % 2, :],
                                        mask_sb[j % 2][:, w, :],
                                        ALU.add).then_inc(s_dve.h, 1)
                    P.on("dve", dve_mask)
                    s_dve.n += 1
                    msk_mark[i] = s_dve.n
                    mask_guard[j] = s_dve.n

                def dve_bm(v, i=i, j=j, sj=s_dve.n):
                    v.wait_ge(s_dve.h, sj)
                    v.wait_ge(s_pe.h, sc_mark[i])
                    v.reduce_max(stats[:, ST_BM + j:ST_BM + j + 1],
                                 ps_a[:, i % 2, :], axis=AX.X).then_inc(s_dve.h, 1)
                P.on("dve", dve_bm)
                s_dve.n += 1

            def emit_lm(i):
                kb, j, w = updates[i]

                def dve_lm(v, i=i, j=j, sj=s_dve.n):
                    v.wait_ge(s_dve.h, sj)
                    v.wait_ge(s_act.h, exp_mark[i])
                    v.tensor_copy(stats[:, ST_CSV + j:ST_CSV + j + 1],
                                  stats[:, ST_CORR + j:ST_CORR + j + 1])
                    v.drain()
                    v.scalar_tensor_tensor(
                        stats[:, ST_L + j:ST_L + j + 1],
                        stats[:, ST_L + j:ST_L + j + 1],
                        stats[:, ST_CORR + j:ST_CORR + j + 1],
                        stats[:, ST_BS + j:ST_BS + j + 1],
                        op0=ALU.mult, op1=ALU.add)
                    v.drain()
                    v.tensor_scalar_mul(stats[:, ST_M + j:ST_M + j + 1],
                                        stats[:, ST_NM + j:ST_NM + j + 1],
                                        -1.0).then_inc(s_dve.h, 1)
                P.on("dve", dve_lm)
                s_dve.n += 1

            def emit_nm(i, sep):
                kb, j, w = updates[i]

                def dve_nm(v, i=i, j=j, sep=sep, sj=s_dve.n):
                    v.wait_ge(s_dve.h, sj)
                    v.tensor_scalar(
                        stats[:, ST_NM + j:ST_NM + j + 1],
                        stats[:, ST_BM + j:ST_BM + j + 1],
                        stats[:, ST_M + j:ST_M + j + 1], -1.0,
                        op0=ALU.max, op1=ALU.mult,
                    ).then_inc(s_dve.h, 1)
                P.on("dve", dve_nm)
                s_dve.n += 1
                nm_mark[i] = s_dve.n

            def emit_o(i):
                kb, j, w = updates[i]

                def dve_o(v, i=i, j=j, sj=s_dve.n):
                    v.wait_ge(s_dve.h, sj)
                    v.wait_ge(s_pe.h, av_mark[i])
                    v.scalar_tensor_tensor(
                        o_acc[:, j, :], o_acc[:, j, :],
                        stats[:, ST_CSV + j:ST_CSV + j + 1],
                        ps_c[:, :, :].rearrange("pp a b -> pp (a b)"),
                        op0=ALU.mult, op1=ALU.add,
                    ).then_inc(s_dve.h, 1)
                P.on("dve", dve_o)
                s_dve.n += 1
                oup_mark[i] = s_dve.n

            def emit_ptv(i):
                def act_ptv(s, i=i):
                    s.wait_ge(s_pe.h, tr_mark[i])
                    s.copy(
                        pt_sb[i % 2][:, :, :],
                        ps_t[i % 2][:, :512].rearrange(
                            "pp (a b) -> pp a b", a=4),
                    ).then_inc(s_act.h, 1)
                P.on("act", act_ptv)
                s_act.n += 1
                ptv_mark[i] = s_act.n

            def emit_corr_exp(k):
                kb, j, w = updates[k]

                def act_corr(s, k=k, j=j):
                    s.wait_ge(s_dve.h, nm_mark[k])
                    s.activation(stats[:, ST_CORR + j:ST_CORR + j + 1],
                                 stats[:, ST_M + j:ST_M + j + 1], AF.Exp,
                                 bias=stats[:, ST_NM + j:ST_NM + j + 1],
                                 scale=1.0).then_inc(s_act.h, 1)
                P.on("act", act_corr)
                s_act.n += 1

                def act_exp(s, k=k, j=j, w=w):
                    if w is not None:
                        s.wait_ge(s_dve.h, msk_mark[k])
                    s.activation(p_sb[k % 2][:, :], ps_a[:, k % 2, :], AF.Exp,
                                 bias=stats[:, ST_NM + j:ST_NM + j + 1], scale=1.0,
                                 accum_out=stats[:, ST_BS + j:ST_BS + j + 1],
                                 ).then_inc(s_act.h, 1)
                P.on("act", act_exp)
                s_act.n += 1
                exp_mark[k] = s_act.n

            def run_epi_step():
                if not epi_queue:
                    return
                j, step = epi_queue[0]
                epi_queue[0][1] += 1
                if step == 0:
                    def dve_e0(v, j=j, sj=s_dve.n):
                        v.wait_ge(s_dve.h, sj)
                        v.wait_ge(s_dve.h, oup_last[j])
                        v.reciprocal(stats[:, ST_RECIP:ST_RECIP + 1],
                                     stats[:, ST_L + j:ST_L + j + 1]
                                     ).then_inc(s_dve.h, 1)
                    P.on("dve", dve_e0)
                    s_dve.n += 1
                elif step == 1:
                    def dve_e1(v, j=j, sj=s_dve.n):
                        v.wait_ge(s_dve.h, sj)
                        v.wait_ge(s_xq.h, xq_loaded[j])
                        v.scalar_tensor_tensor(
                            o_acc[:, j, :], o_acc[:, j, :],
                            stats[:, ST_RECIP:ST_RECIP + 1], xq_t[:, :],
                            op0=ALU.mult, op1=ALU.add,
                        ).then_inc(s_dve.h, 1)
                    P.on("dve", dve_e1)
                    s_dve.n += 1
                    xq_guard.append(s_dve.n)
                elif step == 2:
                    def dve_e2(v, j=j, sj=s_dve.n):
                        v.wait_ge(s_dve.h, sj)
                        for k in range(4):
                            v.bn_stats(bnbuf[:, 2, k * 6:(k + 1) * 6],
                                       o_acc[:, j, k * 512:(k + 1) * 512])
                        v.drain()
                        v.sem_inc(s_dve.h, 1)
                    P.on("dve", dve_e2)
                    s_dve.n += 1
                elif step == 3:
                    def dve_e3(v, j=j, sj=s_dve.n):
                        v.wait_ge(s_dve.h, sj)
                        v.bn_aggr(stats[:, ST_MEAN:ST_MEAN + 2], bnbuf[:, 2, :])
                        v.drain()
                        v.sem_inc(s_dve.h, 1)
                    P.on("dve", dve_e3)
                    s_dve.n += 1
                    vm = s_dve.n

                    def act_e3(s, j=j, vm=vm):
                        s.wait_ge(s_dve.h, vm)
                        s.activation(stats[:, ST_LNV:ST_LNV + 1],
                                     stats[:, ST_VAR:ST_VAR + 1], AF.Ln,
                                     bias=stats[:, ST_EPS:ST_EPS + 1], scale=1.0)
                        s.drain()
                        s.activation(stats[:, ST_RSTD:ST_RSTD + 1],
                                     stats[:, ST_LNV:ST_LNV + 1], AF.Exp,
                                     bias=0.0, scale=-0.5).then_inc(s_act.h, 1)
                    P.on("act", act_e3)
                    s_act.n += 1
                    rstd_ready[j] = s_act.n
                elif step == 4:
                    hw = h1_written[-1]

                    def dve_e4(v, j=j, hw=hw, sj=s_dve.n):
                        v.wait_ge(s_dve.h, sj)
                        if hw:
                            v.wait_ge(s_sh1.h, hw)
                        v.wait_ge(s_act.h, rstd_ready[j])
                        v.tensor_scalar(
                            h1st[:, :], o_acc[:, j, :],
                            stats[:, ST_MEAN:ST_MEAN + 1],
                            stats[:, ST_RSTD:ST_RSTD + 1],
                            op0=ALU.subtract, op1=ALU.mult,
                        ).then_inc(s_dve.h, 1)
                    P.on("dve", dve_e4)
                    s_dve.n += 1
                    h1_ready[j] = s_dve.n
                elif step == 5:
                    def sp_h1(g, j=j):
                        g.wait_ge(s_dve.h, h1_ready[j])
                        g.dma_start(
                            out=h1_dram[j * 128:(j + 1) * 128, :], in_=h1st[:, :],
                        ).then_inc(s_sh1.h, 16)
                    P.on("pool", sp_h1)
                    s_sh1.n += 16
                    h1_written.append(s_sh1.n)
                    epi_queue.pop(0)

            for i0 in (0, 1):
                kb0, j0, w0 = updates[i0]
                if w0 == 0 and j0 not in mask_loaded:
                    emit_mask_load(j0)
            emit_scores(0)
            if NU > 1:
                emit_scores(1)
            emit_bm(0, need_sep=True)
            emit_nm(0, sep=True)
            emit_corr_exp(0)
            nm_postponed = set()

            for i in range(NU):
                kb, j, w = updates[i]
                if i + 1 < NU:
                    emit_bm(i + 1, need_sep=False)
                if i >= 1:
                    emit_lm(i - 1)
                if i in nm_postponed:
                    emit_nm(i, sep=True)
                if i + 1 < NU:
                    if updates[i + 1][1] == updates[i][1]:
                        nm_postponed.add(i + 1)
                    else:
                        emit_nm(i + 1, sep=True)
                if i >= 1:
                    emit_o(i - 1)
                if i >= 1:
                    emit_ptv(i - 1)
                if i + 1 < NU:
                    emit_corr_exp(i + 1)
                run_epi_step()
                emit_tr(i)
                if i >= 1:
                    emit_av(i - 1)
                if i + 2 < NU:
                    kb2, j2, w2 = updates[i + 2]
                    if w2 == 0 and j2 not in mask_loaded:
                        emit_mask_load(j2)
                    emit_scores(i + 2)
                if kb == ret[j]:
                    emit_xq_load(j)
                    epi_queue.append([j, 0])

            iN = NU - 1
            emit_ptv(iN)
            emit_av(iN)
            emit_lm(iN)
            emit_o(iN)

            for i2 in range(NU):
                kb2, j2, _ = updates[i2]
                if kb2 == ret[j2]:
                    oup_last[j2] = oup_mark[i2]

            while epi_queue:
                run_epi_step()

            if debug:
                def dbg_oa(sp, p=p, dm=s_dve.n):
                    sp.wait_ge(s_dve.h, dm)
                    sp.dma_start(out=dbg["oacc"][p, :, :, :], in_=o_acc[:, :, :]).then_inc(s_sout.h, 16)
                P.on("sp", dbg_oa)
                s_sout.n += 16

                def dbg_h1(sp, p=p, hm=s_sh1.n):
                    sp.wait_ge(s_sh1.h, hm)
                    sp.dma_start(out=dbg["h1"][p, :, :], in_=h1_dram.ap()).then_inc(s_sout.h, 16)
                P.on("sp", dbg_h1)
                s_sout.n += 16

            barrier()

            if not want(f"ffn{p}"):
                return False

            H1_ALL = s_sh1.n

            h1bf_guard = {}
            trv = []
            for tt in range(8):
                g2 = h1bf_guard.get(tt - 2, 0)

                def pool_h1(g, tt=tt, g2=g2, h1a=H1_ALL):
                    g.wait_ge(s_sh1.h, h1a)
                    if g2:
                        g.wait_ge(s_pe.h, g2)
                    g.dma_start(
                        out=gu_blk[tt % 2][:, :, :, :].rearrange(
                            "pp a b c -> pp (a b c)")[:, 0:2048],
                        in_=h1_dram[tt * 128:(tt + 1) * 128, :],
                    ).then_inc(s_h1t2[tt % 2].h, 16)
                P.on("pool", pool_h1)
                s_h1t2[tt % 2].n += 16
                ldm = (tt % 2, s_h1t2[tt % 2].n)
                base = s_pe.n

                def pe_h1t(t, tt=tt, ldm=ldm, base=base):
                    t.wait_ge(s_h1t2[ldm[0]].h, ldm[1])
                    src = gu_blk[tt % 2][:, :, :, :].rearrange(
                        "pp a b c -> pp (a b c)")[:, 0:2048]
                    for hc in range(16):
                        gidx = tt * 16 + hc
                        if gidx >= 2:
                            t.wait_ge(s_act.h, trv[gidx - 2])
                        tr = t.transpose(ps_t[hc % 2][:, 0:128],
                                         src[:, hc * 128:(hc + 1) * 128],
                                         ident[:, :])
                        tr.then_inc(s_pe.h, 1)
                P.on("pe", pe_h1t)
                s_pe.n += 16
                h1bf_guard[tt] = s_pe.n

                for hc in range(16):
                    def act_h1t(s, tt=tt, hc=hc, pm=base + hc + 1):
                        s.wait_ge(s_pe.h, pm)
                        s.copy(h1T[:, hc, tt * 128:(tt + 1) * 128],
                               ps_t[hc % 2][:, 0:128]).then_inc(s_act.h, 1)
                    P.on("act", act_h1t)
                    s_act.n += 1
                    trv.append(s_act.n)
            H1T_ACT = s_act.n

            gu_ld = {}
            gu_pe = {}
            silu_m = {}
            u1_m = {}
            hdn_m = {}
            for j in range(44):
                g2 = gu_pe.get(j - 2, 0) if j >= 2 else h1bf_guard[6 + j]

                def pool_gu(g, j=j, g2=g2):
                    if g2:
                        g.wait_ge(s_pe.h, g2)
                    g.dma_start(out=gu_blk[j % 2][:, 0, :, :],
                                in_=d_gateT[j, :, :, :]).then_inc(s_gu2[j % 2].h, 16)
                    g.dma_start(out=gu_blk[j % 2][:, 1, :, :],
                                in_=d_upT[j, :, :, :]).then_inc(s_gu2[j % 2].h, 16)
                P.on("pool", pool_gu)
                s_gu2[j % 2].n += 32
                gu_ld[j] = (j % 2, s_gu2[j % 2].n)

                sg = silu_m.get(j - 1, 0)
                ug = u1_m.get(j - 1, 0)

                def pe_gu(t, j=j, ldm=gu_ld[j], sg=sg, ug=ug, h1a=H1T_ACT):
                    t.wait_ge(s_gu2[ldm[0]].h, ldm[1])
                    if j == 0:
                        t.wait_ge(s_act.h, h1a)
                    if sg:
                        t.wait_ge(s_act.h, sg)
                    if ug:
                        t.wait_ge(s_dve.h, ug)
                    n = 0
                    for which, ps in ((0, ps_a), (1, ps_c)):
                        for hc in range(16):
                            for half in range(2):
                                n += 1
                                mm = t.matmul(
                                    ps[:, half, :],
                                    gu_blk[j % 2][:, which, hc, :],
                                    h1T[:, hc, half * 512:(half + 1) * 512],
                                    start=(hc == 0), stop=(hc == 15),
                                    skip_group_check=True)
                                if n % 16 == 0 and n != 64:
                                    mm.then_inc(s_junk.h, 1)
                    mm.then_inc(s_pe.h, 1)
                P.on("pe", pe_gu)
                s_junk.n += 3
                s_pe.n += 1
                gu_pe[j] = s_pe.n
                pm = s_pe.n

                hg2 = hdn_m.get(j - 2, 0)

                def act_silu(s, j=j, pm=pm, hg2=hg2):
                    s.wait_ge(s_pe.h, pm)
                    if hg2:
                        s.wait_ge(s_dve.h, hg2)
                    s.activation(silu_sb[j % 2][:, :],
                                 ps_a[:, 0:2, :].rearrange("pp a b -> pp (a b)"),
                                 AF.Silu, scale=gs_sb[:, j:j + 1],
                                 ).then_inc(s_act.h, 1)
                P.on("act", act_silu)
                s_act.n += 1
                silu_m[j] = s_act.n

                def dve_u1(v, j=j, pm=pm, sj=s_dve.n):
                    v.wait_ge(s_dve.h, sj)
                    v.wait_ge(s_pe.h, pm)
                    v.tensor_scalar_mul(
                        u1_sb[j % 2][:, :],
                        ps_c[:, 0:2, :].rearrange("pp a b -> pp (a b)"),
                        us_sb[:, j:j + 1]).then_inc(s_dve.h, 1)
                P.on("dve", dve_u1)
                s_dve.n += 1
                u1_m[j] = s_dve.n

                if j >= 1:
                    jd = j - 1

                    def dve_hdn(v, jd=jd, sj=s_dve.n):
                        v.wait_ge(s_dve.h, sj)
                        v.wait_ge(s_act.h, silu_m[jd])
                        tgt = (hdn_lo[:, jd, :] if jd < 32
                               else hdn_hi[:, jd - 32, :])
                        v.tensor_tensor(tgt, silu_sb[jd % 2][:, :],
                                        u1_sb[jd % 2][:, :],
                                        ALU.mult).then_inc(s_dve.h, 1)
                    P.on("dve", dve_hdn)
                    s_dve.n += 1
                    hdn_m[jd] = s_dve.n

            def dve_hdn_last(v, sj=s_dve.n):
                v.wait_ge(s_dve.h, sj)
                v.wait_ge(s_act.h, silu_m[43])
                v.tensor_tensor(hdn_hi[:, 11, :], silu_sb[43 % 2][:, :],
                                u1_sb[43 % 2][:, :], ALU.mult).then_inc(s_dve.h, 1)
            P.on("dve", dve_hdn_last)
            s_dve.n += 1
            hdn_m[43] = s_dve.n
            HDN_DONE = s_dve.n

            def pool_ds(g, hd=HDN_DONE):
                g.wait_ge(s_dve.h, hd)
                g.dma_start(out=ds_b[:, :], in_=d_ds[:, :]).then_inc(s_ds.h, 16)
            P.on("pool", pool_ds)
            s_ds.n += 16
            DS_READY = s_ds.n

            def hdn_sl(jj, tt):
                return (hdn_lo[:, jj, tt * 128:(tt + 1) * 128] if jj < 32
                        else hdn_hi[:, jj - 32, tt * 128:(tt + 1) * 128])

            dn_pe = {}
            dn_ld = {}
            x2p_m = {}
            x2_done_tt = {}
            x2buf_guard = [0, 0]
            h1sl_guard = [0, 0]
            pairrot = 0
            for qp in range(2):
                for gidx in range(9):
                    glen = DOWN_JG[gidx]
                    j0 = sum(DOWN_JG[:gidx])
                    prevkey = (qp, gidx - 2) if gidx >= 2 else (
                        (qp - 1, 8) if gidx == 0 else (qp - 1, 7))
                    g2 = dn_pe.get(prevkey, 0)

                    def pool_dn(g, qp=qp, gidx=gidx, j0=j0, glen=glen, g2=g2,
                                hd=HDN_DONE):
                        if g2:
                            g.wait_ge(s_pe.h, g2)
                        if qp == 0 and gidx == 0:
                            g.wait_ge(s_dve.h, hd)
                        g.dma_start(
                            out=down_bf[gidx % 2][:, 0:glen, :],
                            in_=d_downT[qp, j0:j0 + glen, :, :].rearrange(
                                "jj pp d -> pp jj d"),
                        ).then_inc(s_dn2[gidx % 2].h, 16)
                    P.on("pool", pool_dn)
                    s_dn2[gidx % 2].n += 16
                    dn_ld[(qp, gidx)] = (gidx % 2, s_dn2[gidx % 2].n)

                    for tt in range(8):
                        bp = pairrot % 3
                        pairrot += 1
                        evg = x2p_m.get((qp, gidx, tt - 3), 0) if tt >= 3 else (
                            x2p_m.get((qp, gidx - 1, 5 + tt), 0) if gidx >= 1 else
                            x2p_m.get((qp - 1, 8, 5 + tt), 0))

                        def pe_dn(t, qp=qp, gidx=gidx, glen=glen, j0=j0, tt=tt,
                                  bp=bp, ldm=dn_ld[(qp, gidx)], evg=evg,
                                  hd=HDN_DONE):
                            t.wait_ge(s_dn2[ldm[0]].h, ldm[1])
                            t.wait_ge(s_dve.h, hd)
                            if evg:
                                t.wait_ge(s_dve.h, evg)
                            n = 0
                            for jl in range(glen):
                                for qh in range(2):
                                    n += 1
                                    ps_dn = (ps_c[:, qh, :] if bp == 0 else
                                             ps_c[:, 2 + qh, :] if bp == 1 else
                                             ps_a[:, qh, :])
                                    mm = t.matmul(
                                        ps_dn,
                                        hdn_sl(j0 + jl, tt),
                                        down_bf[gidx % 2][:, jl,
                                                          qh * 512:(qh + 1) * 512],
                                        start=(jl == 0), stop=(jl == glen - 1),
                                        skip_group_check=True)
                                    if n % 10 == 0 and n != glen * 2:
                                        mm.then_inc(s_junk.h, 1)
                            mm.then_inc(s_pe.h, 1)
                        P.on("pe", pe_dn)
                        s_junk.n += (glen * 2 - 1) // 10
                        s_pe.n += 1
                        dn_pe[(qp, gidx)] = s_pe.n
                        pm = s_pe.n

                        def dve_x2p(v, qp=qp, gidx=gidx, tt=tt, bp=bp, pm=pm,
                                    sj=s_dve.n):
                            v.wait_ge(s_dve.h, sj)
                            v.wait_ge(s_pe.h, pm)
                            src = (ps_c[:, 0:2, :] if bp == 0 else
                                   ps_c[:, 2:4, :] if bp == 1 else
                                   ps_a[:, 0:2, :]).rearrange(
                                "pp a b -> pp (a b)")
                            if gidx == 0:
                                v.tensor_copy(x2part[:, tt, :], src)
                            else:
                                v.tensor_tensor(x2part[:, tt, :],
                                                x2part[:, tt, :], src, ALU.add)
                            v.drain()
                            v.sem_inc(s_dve.h, 1)
                        P.on("dve", dve_x2p)
                        s_dve.n += 1
                        x2p_m[(qp, gidx, tt)] = s_dve.n

                for tt in range(8):
                    hg = h1sl_guard[tt % 2]

                    def sp_h1sl(sp, qp=qp, tt=tt, hg=hg, h1a=H1_ALL):
                        sp.wait_ge(s_sh1.h, h1a)
                        if hg:
                            sp.wait_ge(s_dve.h, hg)
                        sp.dma_start(
                            out=h1sl2[tt % 2][:, :],
                            in_=h1_dram[tt * 128:(tt + 1) * 128,
                                        qp * 1024:(qp + 1) * 1024],
                        ).then_inc(s_h1s2[tt % 2].h, 16)
                    P.on("sp", sp_h1sl)
                    s_h1s2[tt % 2].n += 16
                    h1m = s_h1s2[tt % 2].n
                    accg = x2p_m[(qp, 8, tt)]
                    par = tt % 2
                    tgt = qp0sv[tt] if qp == 0 else x2buf[par]
                    xg = x2buf_guard[par] if qp == 1 else 0

                    def dve_x2(v, qp=qp, tt=tt, h1m=h1m, par=par, tgt=tgt,
                               accg=accg, xg=xg, dsr=DS_READY, sj=s_dve.n,
                               b=ST2_BASE[par]):
                        v.wait_ge(s_dve.h, sj)
                        v.wait_ge(s_h1s2[par].h, h1m)
                        v.wait_ge(s_dve.h, accg)
                        v.wait_ge(s_ds.h, dsr)
                        if xg:
                            v.wait_ge(s_sout.h, xg)
                        for qh in range(2):
                            q = 2 * qp + qh
                            v.tensor_tensor(
                                tgt[:, qh * 512:(qh + 1) * 512],
                                x2part[:, tt, qh * 512:(qh + 1) * 512],
                                ds_b[:, q * 512:(q + 1) * 512], ALU.mult)
                        v.drain()
                        v.tensor_tensor(tgt[:, :], tgt[:, :],
                                        h1sl2[par][:, :], ALU.add)
                        v.drain()
                        for k in range(2):
                            v.bn_stats(bnst[:, tt, (2 * qp + k) * 6:
                                            (2 * qp + k + 1) * 6],
                                       tgt[:, k * 512:(k + 1) * 512])
                        if qp == 0:
                            v.drain()
                            v.sem_inc(s_dve.h, 1)
                        else:
                            v.drain()
                            v.bn_aggr(stats[:, b + ST2_MEAN:b + ST2_MEAN + 2],
                                      bnst[:, tt, :])
                            v.drain()
                            v.sem_inc(s_dve.h, 1)
                    P.on("dve", dve_x2)
                    s_dve.n += 1
                    h1sl_guard[tt % 2] = s_dve.n

                    if qp == 1:
                        vm = s_dve.n

                        def act_ln2(s, vm=vm, b=ST2_BASE[par]):
                            s.wait_ge(s_dve.h, vm)
                            s.activation(
                                stats[:, b + ST2_LNV:b + ST2_LNV + 1],
                                stats[:, b + ST2_VAR:b + ST2_VAR + 1], AF.Ln,
                                bias=stats[:, ST_EPS:ST_EPS + 1], scale=1.0)
                            s.drain()
                            s.activation(
                                stats[:, b + ST2_RSTD:b + ST2_RSTD + 1],
                                stats[:, b + ST2_LNV:b + ST2_LNV + 1], AF.Exp,
                                bias=0.0, scale=-0.5).then_inc(s_act.h, 1)
                        P.on("act", act_ln2)
                        s_act.n += 1
                        am = s_act.n

                        def dve_norm(v, tt=tt, par=par, am=am, sj=s_dve.n,
                                     b=ST2_BASE[par]):
                            v.wait_ge(s_dve.h, sj)
                            v.wait_ge(s_act.h, am)
                            v.tensor_scalar(
                                x2buf[par][:, :], x2buf[par][:, :],
                                stats[:, b + ST2_MEAN:b + ST2_MEAN + 1],
                                stats[:, b + ST2_RSTD:b + ST2_RSTD + 1],
                                op0=ALU.subtract, op1=ALU.mult)
                            v.tensor_scalar(
                                qp0sv[tt][:, :], qp0sv[tt][:, :],
                                stats[:, b + ST2_MEAN:b + ST2_MEAN + 1],
                                stats[:, b + ST2_RSTD:b + ST2_RSTD + 1],
                                op0=ALU.subtract, op1=ALU.mult)
                            v.drain()
                            v.sem_inc(s_dve.h, 1)
                        P.on("dve", dve_norm)
                        s_dve.n += 1
                        nm2 = s_dve.n

                        def sp_out(sp, tt=tt, nm2=nm2, p=p, par=par):
                            sp.wait_ge(s_dve.h, nm2)
                            row0 = (p * 8 + tt) * 128
                            sp.dma_start(
                                out=d_out[row0:row0 + 128, 0:1024],
                                in_=qp0sv[tt][:, :],
                            ).then_inc(s_sout.h, 16)
                            sp.dma_start(
                                out=d_out[row0:row0 + 128, 1024:2048],
                                in_=x2buf[par][:, :],
                            ).then_inc(s_sout.h, 16)
                        P.on("act", sp_out)
                        s_sout.n += 32
                        x2buf_guard[par] = s_sout.n

            barrier()
            return True

        for p in range(2):
            if not want(f"qproj{p}"):
                break
            if not emit_pass(p):
                break

        _emit_final_and_block()
    return nc


def kernel(**inputs):
    in_maps, row_maps = _stage(inputs)
    nc = build_nc(debug=False)
    res = run_bass_kernel_spmd(nc, in_maps, core_ids=list(range(NCORES)))
    out = np.empty((S, E), np.float32)
    for c in range(NCORES):
        out[row_maps[c]] = res.results[c]["out"]
    return out.reshape(1, S, E)
